# version 01
# baseline (speedup 1.0000x reference)
import numpy as np
import ml_dtypes
from contextlib import ExitStack

import concourse.bass as bass
import concourse.mybir as mybir
from concourse.bass_utils import run_bass_kernel_spmd

F32 = mybir.dt.float32
BF16 = mybir.dt.bfloat16
AF = mybir.ActivationFunctionType
ALU = mybir.AluOpType
AX = mybir.AxisListType.X

PI = float(np.pi)
MAGIC = 12582912.0
INV2PI = float(1.0 / (2.0 * np.pi))
TWOPI = float(2.0 * np.pi)

D = 1024
FF = 4096
E = 8
T = 4096
TTA = 512
NTT = T // TTA
TTB = 256
NTTB = T // TTB
DCN = 8
SUB = 64
NSUB = TTA // SUB
CNT_INV = float(1.0 / (16384.0 * 32.0))


def _sched_run(nc, sem, sched):
    def body(eng_name):
        def _(eng):
            for i, (en, fn) in enumerate(sched):
                if en != eng_name:
                    continue
                eng.wait_ge(sem, i)
                last = fn(eng)
                if last is not None:
                    last.then_inc(sem, 1)
                else:
                    eng.sem_inc(sem, 1)
        return _
    with nc.Block() as blk:
        blk.sync(body("s"))
        blk.tensor(body("t"))
        blk.vector(body("v"))
        blk.scalar(body("a"))
        blk.gpsimd(body("g"))


def _act_setids():
    from concourse.hw_specs import get_activation_tables
    tabs = list(get_activation_tables("gen3").values())
    def find(*funcs):
        for i, fs in enumerate(tabs):
            if all(f in fs for f in funcs):
                return i
        raise ValueError(funcs)
    return {
        AF.Sqrt: find(AF.Sqrt, AF.Square, AF.Sign),
        AF.Sigmoid: find(AF.Sigmoid, AF.Arctan, AF.Sign),
        AF.Arctan: find(AF.Sigmoid, AF.Arctan, AF.Sign),
        AF.Ln: find(AF.Ln, AF.Exp),
        AF.Exp: find(AF.Ln, AF.Exp),
        AF.Sin: find(AF.Sin, AF.Silu),
        AF.Silu: find(AF.Sin, AF.Silu),
        AF.Gelu: find(AF.Gelu),
        AF.Gelu_apprx_tanh: find(AF.Gelu_apprx_tanh),
    }


_ACT_WILD = {AF.Square, AF.Sign, AF.Abs, AF.Copy, AF.Identity, AF.Relu,
             AF.Is_finite}


def build_nc(dbg=False):
    nc = bass.Bass("TRN2", target_bir_lowering=False)
    _setid = _act_setids()
    _att = {"cur": None}

    def _tload(a, sid):
        a.add_instruction(mybir.InstLoadActFuncSet(
            name=nc.get_next_instruction_name(),
            act_func_set_id=sid, ins=[], outs=[]))
        _att["cur"] = sid

    def act(a, out, in_, func, **kw):
        if func in _ACT_WILD:
            if _att["cur"] is None:
                _tload(a, _setid[AF.Sqrt])
        elif _setid[func] != _att["cur"]:
            _tload(a, _setid[func])
        return a.activation(out, in_, func, **kw)

    x_d = nc.dram_tensor("x", [T, D], F32, kind="ExternalInput")
    wq_d = nc.dram_tensor("wq", [D, D], F32, kind="ExternalInput")
    wk_d = nc.dram_tensor("wk", [D, D], F32, kind="ExternalInput")
    wv_d = nc.dram_tensor("wv", [D, D], F32, kind="ExternalInput")
    wa_d = nc.dram_tensor("wa", [D, 2 * D], F32, kind="ExternalInput")
    wg_d = nc.dram_tensor("wg", [D, D], F32, kind="ExternalInput")
    wo_d = nc.dram_tensor("wo", [D, D], F32, kind="ExternalInput")
    w1_d = nc.dram_tensor("w1", [D, FF], F32, kind="ExternalInput")
    w2_d = nc.dram_tensor("w2", [FF, D], F32, kind="ExternalInput")
    b1c_d = nc.dram_tensor("b1c", [128, 32], F32, kind="ExternalInput")
    b2b_d = nc.dram_tensor("b2b", [128, D], F32, kind="ExternalInput")
    l1sb_d = nc.dram_tensor("l1sb", [128, D], F32, kind="ExternalInput")
    l1bb_d = nc.dram_tensor("l1bb", [128, D], F32, kind="ExternalInput")
    l2sb_d = nc.dram_tensor("l2sb", [128, D], F32, kind="ExternalInput")
    l2bb_d = nc.dram_tensor("l2bb", [128, D], F32, kind="ExternalInput")
    gnsr_d = nc.dram_tensor("gnsr", [1, D], F32, kind="ExternalInput")
    gnbr_d = nc.dram_tensor("gnbr", [1, D], F32, kind="ExternalInput")
    gwT_d = nc.dram_tensor("gwT", [D, E], F32, kind="ExternalInput")
    guT_d = nc.dram_tensor("guT", [E, D, 2 * FF], F32, kind="ExternalInput")
    dwT_d = nc.dram_tensor("dwT", [E, FF, D], F32, kind="ExternalInput")
    ident_d = nc.dram_tensor("ident", [128, 128], BF16, kind="ExternalInput")
    gmap_d = nc.dram_tensor("gmap", [128, 4], F32, kind="ExternalInput")
    onehot_d = nc.dram_tensor("onehot", [128, 8], F32, kind="ExternalInput")
    bsel_d = nc.dram_tensor("bsel", [128, 2], F32, kind="ExternalInput")

    y_d = nc.dram_tensor("y", [T, D], F32, kind="ExternalOutput")

    stash_d = nc.dram_tensor("stash", [6, DCN, 128, T], F32)
    z_d = nc.dram_tensor("zst", [DCN, 128, T], BF16)
    hact_d = nc.dram_tensor("hact", [E, 32, 128, T], BF16)
    gmoe_d = nc.dram_tensor("gmoe", [32, 128, D], BF16)
    wo_out_d = nc.dram_tensor("wo_out", [32, 128, D], F32)
    summ_in_d = nc.dram_tensor("summ_in", [128, 32], F32)
    summ_ag_d = nc.dram_tensor("summ_ag", [1024, 32], F32)
    stats_in_d = nc.dram_tensor("stats_in", [2, 128], F32)
    stats_out_d = nc.dram_tensor("stats_out", [2, 128], F32)
    crow_d = nc.dram_tensor("crow", [2, D], F32)

    if dbg:
        dbg_stash = nc.dram_tensor("dbg_stash", [6, DCN, 128, T], F32,
                                   kind="ExternalOutput")
        dbg_z = nc.dram_tensor("dbg_z", [DCN, 128, T], BF16,
                               kind="ExternalOutput")
        dbg_xnf = nc.dram_tensor("dbg_xnf", [128, DCN * T], BF16,
                                 kind="ExternalOutput")
        dbg_summ = nc.dram_tensor("dbg_summ", [1024, 32], F32,
                                  kind="ExternalOutput")
        dbg_crow = nc.dram_tensor("dbg_crow", [2, D], F32,
                                  kind="ExternalOutput")
        dbg_wo = nc.dram_tensor("dbg_wo", [32, 128, D], F32,
                                kind="ExternalOutput")
        dbg_gm = nc.dram_tensor("dbg_gm", [32, 128, D], BF16,
                                kind="ExternalOutput")
        dbg_hact = nc.dram_tensor("dbg_hact", [2, 32, 128, 512], BF16,
                                  kind="ExternalOutput")
        dbg_st4 = nc.dram_tensor("dbg_st4", [128, 8 * 24], F32,
                                 kind="ExternalOutput")
        dbg_comb = nc.dram_tensor("dbg_comb", [128, 256], F32,
                                  kind="ExternalOutput")
        dbg_acc = nc.dram_tensor("dbg_acc", [128, 8 * 512], F32,
                                 kind="ExternalOutput")
        dbg_acc0 = nc.dram_tensor("dbg_acc0", [128, 8 * 512], F32,
                                  kind="ExternalOutput")
        dbg_hls = nc.dram_tensor("dbg_hls", [128, 32 * 128], BF16,
                                 kind="ExternalOutput")

    es = ExitStack()
    with es:
        def SEM(name):
            return es.enter_context(nc.semaphore(name))
        sQL = SEM("sQL")
        sQC = SEM("sQC")
        sQS = SEM("sQS")
        sQO = SEM("sQO")
        sCC = SEM("sCC")
        sA1 = SEM("sA1")
        sQG = SEM("sQG")
        sA2 = SEM("sA2")
        sWO = SEM("sWO")
        sB = SEM("sB")
        sGW = [SEM("sGW0"), SEM("sGW1")]
        sPE1 = SEM("sPE1")
        sAC1 = SEM("sAC1")
        sDV1 = SEM("sDV1")
        sHAb = [SEM(f"sHA{i}") for i in range(4)]
        sDW = [SEM("sDW0"), SEM("sDW1")]
        sDH = SEM("sDH")
        sPE2 = SEM("sPE2")
        sDV2 = SEM("sDV2")
        sSI = SEM("sSI")
        sSJ = SEM("sSJ")
        cnt = {"QL": 0, "QC": 0, "QS": 0, "QO": 0, "CC": 0}

        def SB(name, shape, dt):
            return es.enter_context(nc.sbuf_tensor(name, shape, dt))

        mask64 = SB("mask64", [128, TTA], F32)
        ident = SB("identsb", [128, 128], BF16)
        gmap = SB("gmapsb", [128, 4], F32)
        onehot = SB("onehotsb", [128, 8], F32)
        bsel = SB("bselsb", [128, 2], F32)
        combine = SB("combine", [128, 32 * E], F32)
        summ = SB("summ", [128, 32], F32)
        hg = SB("hg", [128, 16], F32)
        stats_acc = SB("stats_acc", [128, 32], F32)
        c1pp = SB("c1pp", [128, 8], F32)
        c2pp = SB("c2pp", [128, 8], F32)
        epsc = SB("epsc", [128, 4], F32)
        st4c_ref = SB("st4c", [128, 8 * 24], F32)

        with nc.Block() as blk:
            @blk.gpsimd
            def _(g):
                g.dma_start(ident[:, :], ident_d[:, :]).then_inc(sQC, 16)
                g.dma_start(gmap[:, :], gmap_d[:, :]).then_inc(sQC, 16)
                g.dma_start(onehot[:, :], onehot_d[:, :]).then_inc(sQC, 16)
                g.dma_start(bsel[:, :], bsel_d[:, :]).then_inc(sQC, 16)
                cnt["QC"] += 64
                g.wait_ge(sQC, cnt["QC"])

            @blk.vector
            def _(v):
                v.memset(mask64[:, :], 1.0)
                v.memset(mask64.ap()[:, 0:TTA:SUB], 0.0)
                v.memset(summ[:, :], 0.0)
                v.memset(summ[:, 0:8], 1.0)
                v.memset(stats_acc[:, :], 0.0)
                v.memset(epsc[:, 0:1], 1e-5)
                v.memset(epsc[:, 1:2], 1e-6)
                v.memset(epsc[:, 2:3], 1e-30)

        xnscope = ExitStack()
        xnf = xnscope.enter_context(nc.sbuf_tensor("xnf", [128, DCN * T], BF16))

        with ExitStack() as a1s:
            def A1SB(name, shape, dt):
                return a1s.enter_context(nc.sbuf_tensor(name, shape, dt))
            wk_sb = A1SB("wk_sb", [128, 8 * D], BF16)
            wv_sb = A1SB("wv_sb", [128, 8 * D], BF16)
            wa_sb = A1SB("wa_sb", [128, 8 * 2 * D], BF16)
            x_tm = A1SB("x_tm", [128, 4 * D], F32)
            xn_tmb = A1SB("xn_tmb", [128, 4 * D], BF16)
            ln_scr = A1SB("ln_scr", [128, D], F32)
            st4 = A1SB("st4", [128, 24], F32)
            b_are = A1SB("b_are", [128, TTA], F32)
            b_aim = A1SB("b_aim", [128, TTA], F32)
            b_kv = A1SB("b_kv", [128, TTA], F32)
            U = [A1SB(f"u{i}", [128, TTA], F32) for i in range(13)]
            csm = A1SB("csm", [128, 120], F32)
            pk = a1s.enter_context(nc.psum_tensor("pk", [128, TTA], F32))
            pv = a1s.enter_context(nc.psum_tensor("pv", [128, TTA], F32))
            par_ps = a1s.enter_context(nc.psum_tensor("par", [128, TTA], F32))
            pai = a1s.enter_context(nc.psum_tensor("pai", [128, TTA], F32))
            tr_ps = a1s.enter_context(nc.psum_tensor("tr_ps", [128, 4096], BF16))

            with nc.Block() as blk:
                @blk.gpsimd
                def _(g):
                    g.dma_start(wk_sb.ap().rearrange("p (c m) -> p c m", c=8),
                                wk_d.ap().rearrange("(c p) m -> p c m", p=128)
                                ).then_inc(sQC, 16)
                    g.dma_start(wv_sb.ap().rearrange("p (c m) -> p c m", c=8),
                                wv_d.ap().rearrange("(c p) m -> p c m", p=128)
                                ).then_inc(sQC, 16)
                    g.dma_start(wa_sb.ap().rearrange("p (c m) -> p c m", c=8),
                                wa_d.ap().rearrange("(c p) m -> p c m", p=128)
                                ).then_inc(sQC, 16)
                    cnt["QC"] += 48
                    g.wait_ge(sQC, cnt["QC"])

            sched = []

            def emit_tt(tt):
                t0 = tt * TTA

                def f_load(eng, t0=t0):
                    eng.dma_start(
                        x_tm.ap().rearrange("p (s d) -> p s d", s=4),
                        x_d[t0:t0 + TTA, :].rearrange("(s p) d -> p s d", p=128)
                    ).then_inc(sQL, 16)
                    cnt["QL"] += 16
                    eng.wait_ge(sQL, cnt["QL"])
                    return None
                sched.append(("s", f_load))

                def f_ln1(v):
                    return v.reduce_sum(
                        st4[:, 0:4],
                        x_tm.ap().rearrange("p (s d) -> p s d", s=4), axis=AX)
                sched.append(("v", f_ln1))

                def f_ln2(a):
                    last = None
                    for s in range(4):
                        last = act(a, ln_scr[:, :], x_tm[:, s * D:(s + 1) * D],
                                            AF.Square,
                                            accum_out=st4[:, 8 + s:9 + s])
                    return last
                sched.append(("a", f_ln2))

                def f_ln3(v):
                    v.tensor_scalar(st4[:, 4:8], st4[:, 0:4], 1.0 / D, None,
                                    ALU.mult)
                    v.tensor_scalar(st4[:, 12:16], st4[:, 8:12], 1.0 / D, None,
                                    ALU.mult)
                    v.tensor_mul(st4[:, 16:20], st4[:, 4:8], st4[:, 4:8])
                    v.memset(st4[:, 0:1], 0.0)
                    return v.tensor_sub(st4[:, 16:20], st4[:, 12:16],
                                        st4[:, 16:20])
                sched.append(("v", f_ln3))

                def f_ln4(a):
                    return act(a, st4[:, 20:24], st4[:, 16:20], AF.Sqrt,
                                        bias=epsc[:, 0:1])
                sched.append(("a", f_ln4))

                def f_ln5(v, tt=tt):
                    v.reciprocal(st4[:, 16:20], st4[:, 20:24])
                    v.memset(st4[:, 0:1], 0.0)
                    for s in range(4):
                        v.tensor_scalar(
                            xn_tmb[:, s * D:(s + 1) * D],
                            x_tm[:, s * D:(s + 1) * D],
                            st4[:, 4 + s:5 + s], st4[:, 16 + s:17 + s],
                            ALU.subtract, ALU.mult)
                    return v.tensor_copy(st4c_ref[:, tt * 24:(tt + 1) * 24],
                                         st4[:, :])
                sched.append(("v", f_ln5))

                def f_tr(t):
                    last = None
                    for dc in range(DCN):
                        for s in range(4):
                            last = t.transpose(
                                tr_ps[:, dc * 512 + s * 128:dc * 512 + (s + 1) * 128],
                                xn_tmb[:, s * D + dc * 128:s * D + (dc + 1) * 128],
                                ident[:, :])
                    return last
                sched.append(("t", f_tr))

                def f_tre(v, tt=tt):
                    last = None
                    for dc in range(DCN):
                        last = v.tensor_copy(
                            xnf[:, dc * T + tt * TTA:dc * T + (tt + 1) * TTA],
                            tr_ps[:, dc * 512:(dc + 1) * 512])
                    return last
                sched.append(("v", f_tre))

                for dc in range(DCN):
                    def f_pe(t, tt=tt, dc=dc):
                        last = None
                        for (ps, wsb, wwid, col0) in (
                                (pk, wk_sb, D, dc * 128),
                                (pv, wv_sb, D, dc * 128),
                                (par_ps, wa_sb, 2 * D, dc * 128),
                                (pai, wa_sb, 2 * D, D + dc * 128)):
                            for kc in range(8):
                                last = t.matmul(
                                    ps[:, :],
                                    wsb[:, kc * wwid + col0:kc * wwid + col0 + 128],
                                    xnf[:, kc * T + tt * TTA:
                                        kc * T + (tt + 1) * TTA],
                                    start=(kc == 0), stop=(kc == 7))
                        return last
                    sched.append(("t", f_pe))

                    def f_d1(v):
                        v.tensor_copy(U[10][:, :], pk[:, :])
                        v.tensor_mul(b_kv[:, :], U[10][:, :], pv[:, :])
                        v.tensor_copy(b_are[:, :], par_ps[:, :])
                        return v.tensor_copy(b_aim[:, :], pai[:, :])
                    sched.append(("v", f_d1))

                    def f_a1(a):
                        act(a, U[1][:, :], b_are[:, :], AF.Sign)
                        act(a, U[2][:, :], b_aim[:, :], AF.Sign)
                        act(a, U[5][:, :], b_are[:, :], AF.Square)
                        act(a, U[6][:, :], b_aim[:, :], AF.Square)
                        return act(a, U[3][:, :], U[1][:, :], AF.Square)
                    sched.append(("a", f_a1))

                    def f_d2(v):
                        v.tensor_add(U[0][:, :], U[5][:, :], U[6][:, :])
                        v.tensor_scalar(U[3][:, :], U[3][:, :], -1e-37, 1e-37,
                                        ALU.mult, ALU.add)
                        v.tensor_add(U[1][:, :], b_are[:, :], U[3][:, :])
                        v.reciprocal(U[3][:, :], U[1][:, :])
                        return v.tensor_mul(U[3][:, :], b_aim[:, :], U[3][:, :])
                    sched.append(("v", f_d2))

                    def f_a2(a):
                        act(a, U[0][:, :], U[0][:, :], AF.Sqrt,
                                     bias=epsc[:, 2:3])
                        return act(a, U[4][:, :], U[1][:, :], AF.Sign)
                    sched.append(("a", f_a2))

                    def f_a3(a):
                        act(a, U[0][:, :], U[0][:, :], AF.Sigmoid)
                        return act(a, U[5][:, :], U[3][:, :], AF.Arctan)
                    sched.append(("a", f_a3))

                    def f_d3(v):
                        v.tensor_scalar(U[4][:, :], U[4][:, :], -PI / 2, PI / 2,
                                        ALU.mult, ALU.add)
                        v.tensor_mul(U[2][:, :], U[2][:, :], U[4][:, :])
                        return v.tensor_add(U[5][:, :], U[5][:, :], U[2][:, :])
                    sched.append(("v", f_d3))

                    def f_a4(a):
                        return act(a, U[0][:, :], U[0][:, :], AF.Ln)
                    sched.append(("a", f_a4))

                    def f_d4(v):
                        v.tensor_tensor_scan(U[1][:, :], mask64[:, :], U[0][:, :],
                                             0.0, ALU.mult, ALU.add)
                        v.tensor_tensor_scan(U[3][:, :], mask64[:, :], U[5][:, :],
                                             0.0, ALU.mult, ALU.add)
                        v.tensor_scalar(U[4][:, :], U[3][:, :], INV2PI, MAGIC,
                                        ALU.mult, ALU.add)
                        v.tensor_scalar(U[4][:, :], U[4][:, :], -MAGIC, None,
                                        ALU.add)
                        return v.scalar_tensor_tensor(U[4][:, :], U[4][:, :],
                                                      -TWOPI, U[3][:, :],
                                                      ALU.mult, ALU.add)
                    sched.append(("v", f_d4))

                    def f_a5(a):
                        act(a, U[6][:, :], U[4][:, :], AF.Abs)
                        act(a, U[7][:, :], U[1][:, :], AF.Exp, scale=-1.0)
                        return act(a, U[1][:, :], U[1][:, :], AF.Exp)
                    sched.append(("a", f_a5))

                    def f_d5(v):
                        return v.tensor_scalar(U[6][:, :], U[6][:, :], -1.0,
                                               PI / 2, ALU.mult, ALU.add)
                    sched.append(("v", f_d5))

                    def f_a6(a):
                        act(a, U[4][:, :], U[4][:, :], AF.Sin)
                        return act(a, U[6][:, :], U[6][:, :], AF.Sin)
                    sched.append(("a", f_a6))

                    def f_d6(v, dc=dc):
                        c = csm
                        v.tensor_mul(U[3][:, :], U[7][:, :], b_kv[:, :])
                        v.tensor_mul(U[8][:, :], U[3][:, :], U[6][:, :])
                        v.tensor_mul(U[9][:, :], U[3][:, :], U[4][:, :])
                        v.reduce_sum(c[:, 0:8],
                                     U[8].ap().rearrange("p (c j) -> p c j", c=8),
                                     axis=AX)
                        v.reduce_sum(c[:, 8:16],
                                     U[9].ap().rearrange("p (c j) -> p c j", c=8),
                                     axis=AX)
                        eL63 = U[1].ap()[:, SUB - 1:TTA:SUB]
                        cos63 = U[6].ap()[:, SUB - 1:TTA:SUB]
                        sin63 = U[4].ap()[:, SUB - 1:TTA:SUB]
                        v.tensor_mul(c[:, 16:24], eL63, cos63)
                        v.tensor_mul(c[:, 24:32], eL63, sin63)
                        v.tensor_copy(c[:, 48:49], summ[:, 16 + dc:17 + dc])
                        v.tensor_copy(c[:, 57:58], summ[:, 24 + dc:25 + dc])
                        v.tensor_copy(c[:, 66:67], summ[:, 0 + dc:1 + dc])
                        v.tensor_copy(c[:, 75:76], summ[:, 8 + dc:9 + dc])
                        v.tensor_mul(c[:, 84:92], c[:, 16:24], c[:, 0:8])
                        v.tensor_mul(c[:, 32:40], c[:, 24:32], c[:, 8:16])
                        v.tensor_mul(c[:, 96:104], c[:, 24:32], c[:, 0:8])
                        v.tensor_mul(c[:, 40:48], c[:, 16:24], c[:, 8:16])
                        v.tensor_add(c[:, 32:40], c[:, 84:92], c[:, 32:40])
                        v.tensor_sub(c[:, 40:48], c[:, 96:104], c[:, 40:48])
                        for cc in range(NSUB):
                            Aer = c[:, 16 + cc:17 + cc]
                            Aei = c[:, 24 + cc:25 + cc]
                            Hr = c[:, 48 + cc:49 + cc]
                            Hi = c[:, 57 + cc:58 + cc]
                            Pr = c[:, 66 + cc:67 + cc]
                            Pi = c[:, 75 + cc:76 + cc]
                            p1 = c[:, 104:105]; p2 = c[:, 105:106]
                            p3 = c[:, 106:107]; p4 = c[:, 107:108]
                            a1 = c[:, 108:109]; a2 = c[:, 109:110]
                            b1 = c[:, 110:111]; b2 = c[:, 111:112]
                            tA = c[:, 112:113]; tB = c[:, 113:114]
                            v.tensor_mul(p1, Pr, Aer)
                            v.tensor_mul(p2, Pi, Aei)
                            v.tensor_mul(p3, Pr, Aei)
                            v.tensor_mul(p4, Pi, Aer)
                            v.tensor_mul(a1, Aer, Hr)
                            v.tensor_mul(a2, Aei, Hi)
                            v.tensor_mul(b1, Aer, Hi)
                            v.tensor_mul(b2, Aei, Hr)
                            v.tensor_sub(c[:, 67 + cc:68 + cc], p1, p2)
                            v.tensor_add(c[:, 76 + cc:77 + cc], p3, p4)
                            v.tensor_sub(tA, a1, a2)
                            v.tensor_add(tB, b1, b2)
                            v.tensor_add(c[:, 49 + cc:50 + cc], tA,
                                         c[:, 32 + cc:33 + cc])
                            v.tensor_add(c[:, 58 + cc:59 + cc], tB,
                                         c[:, 40 + cc:41 + cc])
                        v.tensor_copy(summ[:, 0 + dc:1 + dc], c[:, 74:75])
                        v.tensor_copy(summ[:, 8 + dc:9 + dc], c[:, 83:84])
                        v.tensor_copy(summ[:, 16 + dc:17 + dc], c[:, 56:57])
                        v.tensor_copy(summ[:, 24 + dc:25 + dc], c[:, 65:66])
                        j0r = U[8].ap()[:, 0:TTA:SUB]
                        j0i = U[9].ap()[:, 0:TTA:SUB]
                        v.tensor_add(j0r, j0r, c[:, 48:56])
                        v.tensor_sub(j0i, j0i, c[:, 57:65])
                        v.tensor_tensor_scan(U[7][:, :], mask64[:, :], U[8][:, :],
                                             0.0, ALU.mult, ALU.add)
                        v.tensor_tensor_scan(U[3][:, :], mask64[:, :], U[9][:, :],
                                             0.0, ALU.mult, ALU.add)
                        v.tensor_mul(U[10][:, :], U[1][:, :], U[6][:, :])
                        v.tensor_mul(U[11][:, :], U[1][:, :], U[4][:, :])
                        v.tensor_mul(U[0][:, :], U[10][:, :], U[7][:, :])
                        v.tensor_mul(U[5][:, :], U[11][:, :], U[3][:, :])
                        v.tensor_add(U[0][:, :], U[0][:, :], U[5][:, :])
                        v.tensor_mul(U[5][:, :], U[11][:, :], U[7][:, :])
                        v.tensor_mul(U[6][:, :], U[10][:, :], U[3][:, :])
                        v.tensor_sub(U[5][:, :], U[5][:, :], U[6][:, :])
                        APrb = c.ap()[:, 66:74].unsqueeze(2).broadcast_to(
                            (128, 8, SUB))
                        APib = c.ap()[:, 75:83].unsqueeze(2).broadcast_to(
                            (128, 8, SUB))
                        Ar3 = U[10].ap().rearrange("p (c j) -> p c j", c=8)
                        Ai3 = U[11].ap().rearrange("p (c j) -> p c j", c=8)
                        t4v = U[4].ap().rearrange("p (c j) -> p c j", c=8)
                        t6v = U[6].ap().rearrange("p (c j) -> p c j", c=8)
                        t12v = U[12].ap().rearrange("p (c j) -> p c j", c=8)
                        v.tensor_tensor(t4v, Ar3, APrb, ALU.mult)
                        v.tensor_tensor(t6v, Ai3, APib, ALU.mult)
                        v.tensor_sub(U[4][:, :], U[4][:, :], U[6][:, :])
                        v.tensor_tensor(t6v, Ar3, APib, ALU.mult)
                        v.tensor_tensor(t12v, Ai3, APrb, ALU.mult)
                        return v.tensor_add(U[6][:, :], U[6][:, :], U[12][:, :])
                    sched.append(("v", f_d6))

                    def f_st(g, tt=tt, dc=dc):
                        t0 = tt * TTA
                        g.dma_start(stash_d[0, dc, :, t0:t0 + TTA],
                                    U[0][:, :]).then_inc(sQS, 16)
                        g.dma_start(stash_d[1, dc, :, t0:t0 + TTA],
                                    U[5][:, :]).then_inc(sQS, 16)
                        g.dma_start(stash_d[2, dc, :, t0:t0 + TTA],
                                    U[4][:, :]).then_inc(sQS, 16)
                        g.dma_start(stash_d[3, dc, :, t0:t0 + TTA],
                                    U[6][:, :]).then_inc(sQS, 16)
                        cnt["QS"] += 64
                        g.wait_ge(sQS, cnt["QS"])
                        return None
                    sched.append(("g", f_st))

            for tt in range(NTT):
                emit_tt(tt)
            _sched_run(nc, sA1, sched)

        with ExitStack() as x1s:
            agp = x1s.enter_context(nc.sbuf_tensor("agp", [128, 256], F32))
            prr = x1s.enter_context(nc.sbuf_tensor("prr", [128, 64], F32))
            pri = x1s.enter_context(nc.sbuf_tensor("pri", [128, 64], F32))
            mtmp = x1s.enter_context(nc.sbuf_tensor("mtmp", [128, 32], F32))
            selb = x1s.enter_context(nc.sbuf_tensor("selb", [128, 64], F32))

            with nc.Block() as blk:
                @blk.gpsimd
                def _(g):
                    g.dma_start(summ_in_d[:, :], summ[:, :]).then_inc(sQS, 16)
                    cnt["QS"] += 16
                    g.wait_ge(sQS, cnt["QS"])
                    g.collective_compute(
                        "AllGather", ALU.bypass,
                        replica_groups=[list(range(8))],
                        ins=[summ_in_d.ap().opt()],
                        outs=[summ_ag_d.ap().opt()]).then_inc(sCC, 1)
                    cnt["CC"] += 1
                    g.wait_ge(sCC, cnt["CC"])

            with nc.Block() as blk:
                @blk.sync
                def _(s):
                    s.dma_start(
                        agp.ap().rearrange("p (r c) -> p r c", r=8),
                        summ_ag_d.ap().rearrange("(r p) c -> p r c", p=128)
                    ).then_inc(sQL, 16)
                    cnt["QL"] += 16
                    s.wait_ge(sQL, cnt["QL"])

            with nc.Block() as blk:
                @blk.vector
                def _(v):
                    v.memset(prr[:, :], 0.0)
                    v.memset(pri[:, :], 0.0)
                    for r in range(1, 8):
                        if r == 4:
                            continue
                        co = (r - 1) * 32
                        def colA(dc):
                            return agp[:, co + dc:co + dc + 1]
                        def colAi(dc):
                            return agp[:, co + 8 + dc:co + 8 + dc + 1]
                        def colH(dc):
                            return agp[:, co + 16 + dc:co + 16 + dc + 1]
                        def colHi(dc):
                            return agp[:, co + 24 + dc:co + 24 + dc + 1]
                        for dc in range(DCN):
                            v.tensor_mul(mtmp[:, dc:dc + 1], colA(dc),
                                         prr[:, dc * 8 + r - 1:dc * 8 + r])
                        for dc in range(DCN):
                            v.tensor_mul(mtmp[:, 8 + dc:9 + dc], colAi(dc),
                                         pri[:, dc * 8 + r - 1:dc * 8 + r])
                        for dc in range(DCN):
                            v.tensor_mul(mtmp[:, 16 + dc:17 + dc], colA(dc),
                                         pri[:, dc * 8 + r - 1:dc * 8 + r])
                        for dc in range(DCN):
                            v.tensor_mul(mtmp[:, 24 + dc:25 + dc], colAi(dc),
                                         prr[:, dc * 8 + r - 1:dc * 8 + r])
                        for dc in range(DCN):
                            v.tensor_sub(mtmp[:, dc:dc + 1], mtmp[:, dc:dc + 1],
                                         mtmp[:, 8 + dc:9 + dc])
                        for dc in range(DCN):
                            v.tensor_add(mtmp[:, 16 + dc:17 + dc],
                                         mtmp[:, 16 + dc:17 + dc],
                                         mtmp[:, 24 + dc:25 + dc])
                        for dc in range(DCN):
                            v.tensor_add(prr[:, dc * 8 + r:dc * 8 + r + 1],
                                         mtmp[:, dc:dc + 1], colH(dc))
                        for dc in range(DCN):
                            v.tensor_add(pri[:, dc * 8 + r:dc * 8 + r + 1],
                                         mtmp[:, 16 + dc:17 + dc], colHi(dc))
                    for dc in range(DCN):
                        v.tensor_mul(selb[:, dc * 8:(dc + 1) * 8],
                                     prr[:, dc * 8:(dc + 1) * 8], onehot[:, :])
                    for dc in range(DCN):
                        v.reduce_sum(hg[:, 2 * dc:2 * dc + 1],
                                     selb[:, dc * 8:(dc + 1) * 8], axis=AX)
                    for dc in range(DCN):
                        v.tensor_mul(selb[:, dc * 8:(dc + 1) * 8],
                                     pri[:, dc * 8:(dc + 1) * 8], onehot[:, :])
                    for dc in range(DCN):
                        v.reduce_sum(hg[:, 2 * dc + 1:2 * dc + 2],
                                     selb[:, dc * 8:(dc + 1) * 8], axis=AX)

        with ExitStack() as qgs:
            def QGSB(name, shape, dt):
                return qgs.enter_context(nc.sbuf_tensor(name, shape, dt))
            wq_sb = QGSB("wq_sb", [128, 8 * D], BF16)
            wg_sb = QGSB("wg_sb", [128, 8 * D], BF16)
            gwt_sb = QGSB("gwt_sb", [128, 8 * E], BF16)
            qtile = QGSB("qtile", [128, TTA], F32)
            sgtile = QGSB("sgtile", [128, TTA], F32)
            rt = QGSB("rt", [128, 64], F32)
            pq = qgs.enter_context(nc.psum_tensor("pq", [128, TTA], F32))
            pg = qgs.enter_context(nc.psum_tensor("pg", [128, TTA], F32))
            pr = qgs.enter_context(nc.psum_tensor("pr", [128, 16], F32))

            with nc.Block() as blk:
                @blk.gpsimd
                def _(g):
                    g.dma_start(wq_sb.ap().rearrange("p (c m) -> p c m", c=8),
                                wq_d.ap().rearrange("(c p) m -> p c m", p=128)
                                ).then_inc(sQC, 16)
                    g.dma_start(wg_sb.ap().rearrange("p (c m) -> p c m", c=8),
                                wg_d.ap().rearrange("(c p) m -> p c m", p=128)
                                ).then_inc(sQC, 16)
                    g.dma_start(gwt_sb.ap().rearrange("p (c e) -> p c e", c=8),
                                gwT_d.ap().rearrange("(c p) e -> p c e", p=128)
                                ).then_inc(sQC, 16)
                    cnt["QC"] += 48
                    g.wait_ge(sQC, cnt["QC"])

            sched = []

            def emit_qg(tt):
                for dc in range(DCN):
                    def f_pe(t, tt=tt, dc=dc):
                        last = None
                        for (ps, wsb) in ((pq, wq_sb), (pg, wg_sb)):
                            for kc in range(8):
                                last = t.matmul(
                                    ps[:, :],
                                    wsb[:, kc * D + dc * 128:
                                        kc * D + dc * 128 + 128],
                                    xnf[:, kc * T + tt * TTA:
                                        kc * T + (tt + 1) * TTA],
                                    start=(kc == 0), stop=(kc == 7))
                        return last
                    sched.append(("t", f_pe))

                    def f_ev(v):
                        return v.tensor_copy(qtile[:, :], pq[:, :])
                    sched.append(("v", f_ev))

                    def f_si(a, dc=dc):
                        return act(a, sgtile[:, :], pg[:, :], AF.Silu)
                    sched.append(("a", f_si))

                    def f_st(g, tt=tt, dc=dc):
                        t0 = tt * TTA
                        g.dma_start(stash_d[4, dc, :, t0:t0 + TTA],
                                    qtile[:, :]).then_inc(sQS, 16)
                        g.dma_start(stash_d[5, dc, :, t0:t0 + TTA],
                                    sgtile[:, :]).then_inc(sQS, 16)
                        cnt["QS"] += 32
                        g.wait_ge(sQS, cnt["QS"])
                        return None
                    sched.append(("g", f_st))

                for sp in range(2):
                    def f_rpe(t, tt=tt, sp=sp):
                        last = None
                        for k in range(2):
                            s = sp * 2 + k
                            for kc in range(8):
                                last = t.matmul(
                                    pr[:, k * 8:(k + 1) * 8],
                                    xnf[:, kc * T + tt * TTA + s * 128:
                                        kc * T + tt * TTA + (s + 1) * 128],
                                    gwt_sb[:, kc * E:(kc + 1) * E],
                                    start=(kc == 0), stop=(kc == 7))
                        return last
                    sched.append(("t", f_rpe))

                    def f_rm(v):
                        v.reduce_max(rt[:, 0:1], pr[:, 0:8], axis=AX,
                                     negate=True)
                        return v.reduce_max(rt[:, 32:33], pr[:, 8:16], axis=AX,
                                            negate=True)
                    sched.append(("v", f_rm))

                    def f_re(a, tt=tt, sp=sp):
                        act(a, rt[:, 8:16], pr[:, 0:8], AF.Exp,
                                     bias=rt[:, 0:1])
                        return act(a, rt[:, 40:48], pr[:, 8:16], AF.Exp,
                                            bias=rt[:, 32:33])
                    sched.append(("a", f_re))

                    def f_rc(v, tt=tt, sp=sp):
                        def ops(B, g128):
                            yield lambda: v.reduce_sum(rt[:, B + 1:B + 2],
                                                       rt[:, B + 8:B + 16],
                                                       axis=AX)
                            yield lambda: v.reciprocal(rt[:, B + 2:B + 3],
                                                       rt[:, B + 1:B + 2])
                            yield lambda: v.tensor_scalar(
                                rt[:, B + 8:B + 16], rt[:, B + 8:B + 16],
                                rt[:, B + 2:B + 3], None, ALU.mult)
                            yield lambda: v.reduce_max(rt[:, B + 3:B + 4],
                                                       rt[:, B + 8:B + 16],
                                                       axis=AX)
                            yield lambda: v.tensor_scalar(
                                rt[:, B + 16:B + 24], rt[:, B + 8:B + 16],
                                rt[:, B + 3:B + 4], None, ALU.is_ge)
                            yield lambda: v.tensor_scalar(
                                rt[:, B + 24:B + 32], rt[:, B + 16:B + 24],
                                -1.0, 1.0, ALU.mult, ALU.add)
                            yield lambda: v.tensor_mul(
                                rt[:, B + 24:B + 32], rt[:, B + 8:B + 16],
                                rt[:, B + 24:B + 32])
                            yield lambda: v.reduce_max(rt[:, B + 4:B + 5],
                                                       rt[:, B + 24:B + 32],
                                                       axis=AX)
                            yield lambda: v.tensor_scalar(
                                rt[:, B + 24:B + 32], rt[:, B + 24:B + 32],
                                rt[:, B + 4:B + 5], None, ALU.is_ge)
                            yield lambda: v.tensor_add(rt[:, B + 5:B + 6],
                                                       rt[:, B + 3:B + 4],
                                                       rt[:, B + 4:B + 5])
                            yield lambda: v.reciprocal(rt[:, B + 6:B + 7],
                                                       rt[:, B + 5:B + 6])
                            yield lambda: v.tensor_mul(rt[:, B + 3:B + 4],
                                                       rt[:, B + 3:B + 4],
                                                       rt[:, B + 6:B + 7])
                            yield lambda: v.tensor_mul(rt[:, B + 4:B + 5],
                                                       rt[:, B + 4:B + 5],
                                                       rt[:, B + 6:B + 7])
                            yield lambda: v.tensor_scalar(
                                rt[:, B + 16:B + 24], rt[:, B + 16:B + 24],
                                rt[:, B + 3:B + 4], None, ALU.mult)
                            yield lambda: v.tensor_scalar(
                                rt[:, B + 24:B + 32], rt[:, B + 24:B + 32],
                                rt[:, B + 4:B + 5], None, ALU.mult)
                            yield lambda: v.tensor_add(
                                combine[:, g128 * 8:(g128 + 1) * 8],
                                rt[:, B + 16:B + 24], rt[:, B + 24:B + 32])
                        g0 = tt * 4 + sp * 2
                        it0 = ops(0, g0)
                        it1 = ops(32, g0 + 1)
                        last = None
                        for o0, o1 in zip(it0, it1):
                            last = o0()
                            last = o1()
                        return last
                    sched.append(("v", f_rc))

            for tt in range(NTT):
                emit_qg(tt)
            _sched_run(nc, sQG, sched)

        with ExitStack() as a2s:
            def A2SB(name, shape, dt):
                return a2s.enter_context(nc.sbuf_tensor(name, shape, dt))
            l_h0r = A2SB("l_h0r", [128, TTA], F32)
            l_h0i = A2SB("l_h0i", [128, TTA], F32)
            l_acr = A2SB("l_acr", [128, TTA], F32)
            l_aci = A2SB("l_aci", [128, TTA], F32)
            l_q = A2SB("l_q", [128, TTA], F32)
            l_sg = A2SB("l_sg", [128, TTA], F32)
            w_1 = A2SB("w_1", [128, TTA], F32)
            w_2 = A2SB("w_2", [128, TTA], F32)
            w_3 = A2SB("w_3", [128, TTA], F32)
            zt = A2SB("zt", [128, TTA], BF16)
            acc4 = A2SB("acc4", [128, 4], F32)

            sched = []

            def emit_a2(tt, dc):
                t0 = tt * TTA

                def f_l(s, dc=dc, t0=t0):
                    for qi, dst in ((0, l_h0r), (1, l_h0i), (2, l_acr),
                                    (3, l_aci), (4, l_q), (5, l_sg)):
                        s.dma_start(dst[:, :],
                                    stash_d[qi, dc, :, t0:t0 + TTA]
                                    ).then_inc(sQL, 16)
                        cnt["QL"] += 16
                    s.wait_ge(sQL, cnt["QL"])
                    return None
                sched.append(("s", f_l))

                def f_d(v, dc=dc):
                    hgr = hg[:, 2 * dc:2 * dc + 1]
                    hgi = hg[:, 2 * dc + 1:2 * dc + 2]
                    v.tensor_scalar(w_1[:, :], l_acr[:, :], hgr, None, ALU.mult)
                    v.tensor_scalar(w_2[:, :], l_aci[:, :], hgi, None, ALU.mult)
                    v.tensor_sub(w_1[:, :], w_1[:, :], w_2[:, :])
                    v.tensor_add(w_1[:, :], w_1[:, :], l_h0r[:, :])
                    v.tensor_scalar(w_2[:, :], l_acr[:, :], hgi, None, ALU.mult)
                    v.tensor_scalar(w_3[:, :], l_aci[:, :], hgr, None, ALU.mult)
                    v.tensor_add(w_2[:, :], w_2[:, :], w_3[:, :])
                    v.tensor_add(w_2[:, :], w_2[:, :], l_h0i[:, :])
                    v.tensor_mul(w_1[:, :], l_q[:, :], w_1[:, :])
                    v.tensor_mul(w_2[:, :], l_q[:, :], w_2[:, :])
                    return v.tensor_mul(zt[:, :], w_1[:, :], l_sg[:, :])
                sched.append(("v", f_d))

                def f_a(a):
                    act(a, w_3[:, :], w_1[:, :], AF.Copy,
                                 accum_out=acc4[:, 0:1])
                    act(a, w_3[:, :], w_2[:, :], AF.Copy,
                                 accum_out=acc4[:, 1:2])
                    act(a, w_3[:, :], w_1[:, :], AF.Square,
                                 accum_out=acc4[:, 2:3])
                    return act(a, w_3[:, :], w_2[:, :], AF.Square,
                                        accum_out=acc4[:, 3:4])
                sched.append(("a", f_a))

                def f_d2(v, dc=dc):
                    return v.tensor_add(stats_acc[:, dc * 4:(dc + 1) * 4],
                                        stats_acc[:, dc * 4:(dc + 1) * 4],
                                        acc4[:, :])
                sched.append(("v", f_d2))

                def f_s(g, dc=dc, t0=t0):
                    g.dma_start(z_d[dc, :, t0:t0 + TTA], zt[:, :]
                                ).then_inc(sQS, 16)
                    cnt["QS"] += 16
                    g.wait_ge(sQS, cnt["QS"])
                    return None
                sched.append(("g", f_s))

            for tt in range(NTT):
                for dc in range(DCN):
                    emit_a2(tt, dc)
            _sched_run(nc, sA2, sched)

        with ExitStack() as x2s:
            sg43 = x2s.enter_context(nc.sbuf_tensor("sg43", [4, 4], F32))
            sg43b = x2s.enter_context(nc.sbuf_tensor("sg43b", [4, 4], F32))
            ps43 = x2s.enter_context(nc.psum_tensor("ps43", [4, 4], F32))
            row0 = x2s.enter_context(nc.sbuf_tensor("row0", [1, 128], F32))
            row1 = x2s.enter_context(nc.sbuf_tensor("row1", [1, 128], F32))
            grw = x2s.enter_context(nc.sbuf_tensor("grw", [1, 256], F32))
            gnsr = x2s.enter_context(nc.sbuf_tensor("gnsr_sb", [1, D], F32))
            gnbr = x2s.enter_context(nc.sbuf_tensor("gnbr_sb", [1, D], F32))
            c1row = x2s.enter_context(nc.sbuf_tensor("c1row", [1, D], F32))
            c2row = x2s.enter_context(nc.sbuf_tensor("c2row", [1, D], F32))

            for dc in range(DCN):
                with nc.Block() as blk:
                    @blk.tensor
                    def _(t, dc=dc):
                        t.matmul(ps43[:, :], gmap[:, :],
                                 stats_acc[:, dc * 4:(dc + 1) * 4],
                                 start=True, stop=True)
                with nc.Block() as blk:
                    @blk.vector
                    def _(v):
                        v.tensor_scalar(sg43[:, :], ps43[:, :],
                                        bsel[0:4, 0:1], None, ALU.mult)
                        v.tensor_scalar(sg43b[:, :], ps43[:, :],
                                        bsel[0:4, 1:2], None, ALU.mult)
                with nc.Block() as blk:
                    @blk.gpsimd
                    def _(g, dc=dc):
                        g.dma_start(
                            stats_in_d.ap()[0:1, dc * 16:(dc + 1) * 16]
                            .rearrange("o (a b) -> a (o b)", a=4),
                            sg43[:, :]).then_inc(sQS, 16)
                        g.dma_start(
                            stats_in_d.ap()[1:2, dc * 16:(dc + 1) * 16]
                            .rearrange("o (a b) -> a (o b)", a=4),
                            sg43b[:, :]).then_inc(sQS, 16)
                        cnt["QS"] += 32
                        g.wait_ge(sQS, cnt["QS"])

            with nc.Block() as blk:
                @blk.gpsimd
                def _(g):
                    g.collective_compute(
                        "AllReduce", ALU.add,
                        replica_groups=[list(range(8))],
                        ins=[stats_in_d.ap().opt()],
                        outs=[stats_out_d.ap().opt()]).then_inc(sCC, 1)
                    cnt["CC"] += 1
                    g.wait_ge(sCC, cnt["CC"])

            with nc.Block() as blk:
                @blk.sync
                def _(s):
                    s.dma_start(row0[:, :], stats_out_d[0:1, :]).then_inc(sQL, 16)
                    s.dma_start(row1[:, :], stats_out_d[1:2, :]).then_inc(sQL, 16)
                    s.dma_start(gnsr[:, :], gnsr_d[:, :]).then_inc(sQL, 16)
                    s.dma_start(gnbr[:, :], gnbr_d[:, :]).then_inc(sQL, 16)
                    cnt["QL"] += 64
                    s.wait_ge(sQL, cnt["QL"])

            with nc.Block() as blk:
                @blk.vector
                def _(v):
                    v.tensor_scalar(row0[:, :], row0[:, :], bsel[0:1, 0:1],
                                    None, ALU.mult)
                    v.tensor_scalar(row1[:, :], row1[:, :], bsel[0:1, 1:2],
                                    None, ALU.mult)
                    v.tensor_add(row0[:, :], row0[:, :], row1[:, :])
                    r4 = row0.ap().rearrange("o (g j) -> o g j", g=32)
                    v.tensor_tensor(grw.ap()[0:1, 160:192].unsqueeze(2),
                                    r4[:, :, 2:3],
                                    r4[:, :, 3:4], ALU.add)
                    v.tensor_scalar(grw.ap()[0:1, 0:32].unsqueeze(2),
                                    r4[:, :, 0:1], CNT_INV,
                                    None, ALU.mult)
                    v.tensor_scalar(grw.ap()[0:1, 32:64].unsqueeze(2),
                                    r4[:, :, 1:2], CNT_INV,
                                    None, ALU.mult)
                    v.tensor_scalar(grw[0:1, 64:96], grw[0:1, 160:192],
                                    CNT_INV, None, ALU.mult)
                    v.tensor_mul(grw[0:1, 96:128], grw[0:1, 0:32],
                                 grw[0:1, 0:32])
                    v.tensor_mul(grw[0:1, 128:160], grw[0:1, 32:64],
                                 grw[0:1, 32:64])
                    v.tensor_sub(grw[0:1, 64:96], grw[0:1, 64:96],
                                 grw[0:1, 96:128])
                    v.memset(grw[0:1, 224:225], 0.0)
                    v.tensor_sub(grw[0:1, 64:96], grw[0:1, 64:96],
                                 grw[0:1, 128:160])

            with nc.Block() as blk:
                @blk.scalar
                def _(a):
                    act(a, grw[0:1, 96:128], grw[0:1, 64:96], AF.Sqrt,
                                 bias=epsc[0:1, 1:2])
            with nc.Block() as blk:
                @blk.vector
                def _(v):
                    v.reciprocal(grw[0:1, 192:224], grw[0:1, 96:128])
                    rstd_b = grw.ap()[0:1, 192:224].unsqueeze(2).broadcast_to(
                        (1, 32, 32))
                    mre_b = grw.ap()[0:1, 0:32].unsqueeze(2).broadcast_to(
                        (1, 32, 32))
                    g3 = gnsr.ap().rearrange("o (g j) -> o g j", g=32)
                    c13 = c1row.ap().rearrange("o (g j) -> o g j", g=32)
                    c23 = c2row.ap().rearrange("o (g j) -> o g j", g=32)
                    v.memset(grw[0:1, 224:225], 0.0)
                    v.tensor_tensor(c13, g3, rstd_b, ALU.mult)
                    v.memset(grw[0:1, 225:226], 0.0)
                    v.tensor_tensor(c23, c13, mre_b, ALU.mult)
                    v.memset(grw[0:1, 226:227], 0.0)
                    v.tensor_sub(c2row[:, :], gnbr[:, :], c2row[:, :])

            with nc.Block() as blk:
                @blk.gpsimd
                def _(g):
                    g.dma_start(crow_d[0:1, :], c1row[:, :]).then_inc(sQS, 16)
                    g.dma_start(crow_d[1:2, :], c2row[:, :]).then_inc(sQS, 16)
                    cnt["QS"] += 32
                    g.wait_ge(sQS, cnt["QS"])
            with nc.Block() as blk:
                @blk.sync
                def _(s):
                    for dc in range(DCN):
                        s.dma_start(c1pp[:, dc:dc + 1],
                                    crow_d[0:1, dc * 128:(dc + 1) * 128]
                                    .rearrange("o p -> p o")).then_inc(sQL, 16)
                        s.dma_start(c2pp[:, dc:dc + 1],
                                    crow_d[1:2, dc * 128:(dc + 1) * 128]
                                    .rearrange("o p -> p o")).then_inc(sQL, 16)
                        cnt["QL"] += 32
                    s.wait_ge(sQL, cnt["QL"])

        with ExitStack() as wos:
            def WOSB(name, shape, dt):
                return wos.enter_context(nc.sbuf_tensor(name, shape, dt))
            wo_sb = WOSB("wo_sb", [128, 8 * D], BF16)
            zl = WOSB("zl", [128, DCN * TTA], BF16)
            sgl = WOSB("sgl", [128, DCN * TTA], F32)
            mfm = WOSB("mfm", [128, DCN * TTA], BF16)
            mt1 = WOSB("mt1", [128, TTA], F32)
            mt2 = WOSB("mt2", [128, TTA], F32)
            woo = WOSB("woo", [128, 4 * D], F32)
            wo_ps = wos.enter_context(nc.psum_tensor("wo_ps", [128, 4 * D], F32))

            with nc.Block() as blk:
                @blk.gpsimd
                def _(g):
                    g.dma_start(wo_sb.ap().rearrange("p (c m) -> p c m", c=8),
                                wo_d.ap().rearrange("(c p) m -> p c m", p=128)
                                ).then_inc(sQC, 16)
                    cnt["QC"] += 16
                    g.wait_ge(sQC, cnt["QC"])

            sched = []

            def emit_wo(tt):
                t0 = tt * TTA

                def f_l(s, t0=t0):
                    for dc in range(DCN):
                        s.dma_start(zl[:, dc * TTA:(dc + 1) * TTA],
                                    z_d[dc, :, t0:t0 + TTA]).then_inc(sQL, 16)
                        s.dma_start(sgl[:, dc * TTA:(dc + 1) * TTA],
                                    stash_d[5, dc, :, t0:t0 + TTA]
                                    ).then_inc(sQL, 16)
                        cnt["QL"] += 32
                    s.wait_ge(sQL, cnt["QL"])
                    return None
                sched.append(("s", f_l))

                def f_m(v):
                    last = None
                    for dc in range(DCN):
                        v.tensor_scalar(mt1[:, :], zl[:, dc * TTA:(dc + 1) * TTA],
                                        c1pp[:, dc:dc + 1], None, ALU.mult)
                        v.tensor_scalar(mt2[:, :],
                                        sgl[:, dc * TTA:(dc + 1) * TTA],
                                        c2pp[:, dc:dc + 1], None, ALU.mult)
                        last = v.tensor_add(mfm[:, dc * TTA:(dc + 1) * TTA],
                                            mt1[:, :], mt2[:, :])
                    return last
                sched.append(("v", f_m))

                def f_pe(t):
                    last = None
                    for s in range(4):
                        for half in range(2):
                            for dc in range(DCN):
                                last = t.matmul(
                                    wo_ps[:, s * D + half * 512:
                                          s * D + (half + 1) * 512],
                                    mfm[:, dc * TTA + s * 128:
                                        dc * TTA + (s + 1) * 128],
                                    wo_sb[:, dc * D + half * 512:
                                          dc * D + (half + 1) * 512],
                                    start=(dc == 0), stop=(dc == 7))
                    return last
                sched.append(("t", f_pe))

                def f_ev(v):
                    return v.tensor_copy(woo[:, :], wo_ps[:, :])
                sched.append(("v", f_ev))

                def f_s(g, tt=tt):
                    for s in range(4):
                        g.dma_start(wo_out_d[tt * 4 + s, :, :],
                                    woo[:, s * D:(s + 1) * D]).then_inc(sQS, 16)
                        cnt["QS"] += 16
                    g.wait_ge(sQS, cnt["QS"])
                    return None
                sched.append(("g", f_s))

            for tt in range(NTT):
                emit_wo(tt)
            _sched_run(nc, sWO, sched)

        with ExitStack() as gus:
            def GUSB(name, shape, dt):
                return gus.enter_context(nc.sbuf_tensor(name, shape, dt))
            slg = [GUSB(f"slg{i}", [128, 8 * 512], BF16) for i in range(2)]
            slu = [GUSB(f"slu{i}", [128, 8 * 512], BF16) for i in range(2)]
            geltmp = [GUSB(f"gel{i}", [128, TTA], BF16) for i in range(2)]
            hactb = [GUSB(f"hactb{i}", [128, TTA], BF16) for i in range(4)]
            psg = [gus.enter_context(nc.psum_tensor(f"psg{i}", [128, TTA], F32))
                   for i in range(2)]
            psu = [gus.enter_context(nc.psum_tensor(f"psu{i}", [128, TTA], F32))
                   for i in range(2)]

            NIT = E * 8
            GPI = 8 * 4

            def gu_slab_load(g, it):
                e, fb = divmod(it, 8)
                p = it % 2
                g.dma_start(
                    slg[p].ap().rearrange("p (c f) -> p c f", c=8),
                    guT_d[e, :, fb * 512:(fb + 1) * 512]
                    .rearrange("(c p) f -> p c f", p=128)).then_inc(sGW[p], 16)
                g.dma_start(
                    slu[p].ap().rearrange("p (c f) -> p c f", c=8),
                    guT_d[e, :, FF + fb * 512:FF + (fb + 1) * 512]
                    .rearrange("(c p) f -> p c f", p=128)).then_inc(sGW[p], 16)

            with nc.Block() as blk:
                @blk.gpsimd
                def _(g):
                    gu_slab_load(g, 0)
                    gu_slab_load(g, 1)
                    for it in range(NIT):
                        e, fb = divmod(it, 8)
                        for grp in range(GPI):
                            gg = it * GPI + grp
                            g.wait_ge(sDV1, gg + 1)
                            tc, fc = divmod(grp, 4)
                            t0 = tc * TTA
                            g.dma_start(
                                hact_d[e, fb * 4 + fc, :, t0:t0 + TTA],
                                hactb[gg % 4][:, :]).then_inc(sHAb[gg % 4], 16)
                        if it + 2 < NIT:
                            gu_slab_load(g, it + 2)

                @blk.tensor
                def _(t):
                    for it in range(NIT):
                        p = it % 2
                        t.wait_ge(sGW[p], 32 * (it // 2 + 1))
                        for grp in range(GPI):
                            gg = it * GPI + grp
                            tc, fc = divmod(grp, 4)
                            pp = gg % 2
                            if gg >= 2:
                                t.wait_ge(sAC1, gg - 1)
                                t.wait_ge(sDV1, gg - 1)
                            for hc in range(8):
                                t.matmul(psg[pp][:, :],
                                         slg[p][:, hc * 512 + fc * 128:
                                                hc * 512 + (fc + 1) * 128],
                                         xnf[:, hc * T + tc * TTA:
                                             hc * T + (tc + 1) * TTA],
                                         start=(hc == 0), stop=(hc == 7))
                            last = None
                            for hc in range(8):
                                last = t.matmul(
                                    psu[pp][:, :],
                                    slu[p][:, hc * 512 + fc * 128:
                                           hc * 512 + (fc + 1) * 128],
                                    xnf[:, hc * T + tc * TTA:
                                        hc * T + (tc + 1) * TTA],
                                    start=(hc == 0), stop=(hc == 7))
                            last.then_inc(sPE1, 1)

                @blk.scalar
                def _(a):
                    for gg in range(NIT * GPI):
                        pp = gg % 2
                        a.wait_ge(sPE1, gg + 1)
                        act(a, geltmp[pp][:, :], psg[pp][:, :],
                                     AF.Gelu).then_inc(sAC1, 1)

                @blk.vector
                def _(v):
                    for gg in range(NIT * GPI):
                        pp = gg % 2
                        hb = gg % 4
                        v.wait_ge(sAC1, gg + 1)
                        if gg >= 4:
                            v.wait_ge(sHAb[hb], 16 * ((gg - 4) // 4 + 1))
                        v.tensor_mul(hactb[hb][:, :], psu[pp][:, :],
                                     geltmp[pp][:, :]).then_inc(sDV1, 1)

        if dbg:
            with nc.Block() as blk:
                @blk.gpsimd
                def _(g):
                    g.dma_start(dbg_xnf[:, :], xnf[:, :]).then_inc(sQS, 16)
                    g.dma_start(dbg_st4[:, :], st4c_ref[:, :]).then_inc(sQS, 16)
                    cnt["QS"] += 32
                    g.wait_ge(sQS, cnt["QS"])

        xnscope.close()

        with nc.Block() as blk:
            @blk.gpsimd
            def _(g):
                g.wait_ge(sHAb[0], 16 * 512)
                g.wait_ge(sHAb[1], 16 * 512)
                g.wait_ge(sHAb[2], 16 * 512)
                g.wait_ge(sHAb[3], 16 * 512)

        with ExitStack() as dns:
            def DNSB(name, shape, dt):
                return dns.enter_context(nc.sbuf_tensor(name, shape, dt))
            dts = [DNSB(f"dts{i}", [128, 32 * 512], BF16) for i in range(2)]
            hls = [DNSB(f"hls{i}", [128, 32 * 128], BF16) for i in range(2)]
            acc = DNSB("dacc", [128, 8 * 512], F32)
            gmsb = DNSB("gmsb", [128, 8 * 512], BF16)
            psd = [dns.enter_context(nc.psum_tensor(f"psd{i}", [128, 512], F32))
                   for i in range(2)]

            NITD = 2 * 4 * E

            def dt_slab_load(g, it):
                hh = it // (4 * E)
                e = it % E
                p = it % 2
                g.dma_start(
                    dts[p].ap().rearrange("p (c h) -> p c h", c=32),
                    dwT_d[e, :, hh * 512:(hh + 1) * 512]
                    .rearrange("(c p) h -> p c h", p=128)).then_inc(sDW[p], 16)

            with nc.Block() as blk:
                @blk.gpsimd
                def _(g):
                    dt_slab_load(g, 0)
                    dt_slab_load(g, 1)
                    for it in range(NITD - 2):
                        g.wait_ge(sPE2, (it + 1) * 8)
                        dt_slab_load(g, it + 2)

                @blk.sync
                def _(s):
                    first = [True]
                    for it in range(NITD):
                        tcb = (it // E) % 4
                        e = it % E
                        for t8 in range(8):
                            j = it * 8 + t8
                            if j >= 2:
                                s.wait_ge(sPE2, j - 1)
                            t128 = tcb * 8 + t8
                            s.dma_start(
                                hls[j % 2].ap().rearrange("p (c t) -> p c t",
                                                          c=32),
                                hact_d[e, :, :, t128 * 128:(t128 + 1) * 128]
                                .rearrange("c p t -> p c t")).then_inc(sDH, 16)
                            if dbg and first[0]:
                                first[0] = False
                                s.wait_ge(sDH, 16)
                                s.dma_start(dbg_hls[:, :], hls[0][:, :]
                                            ).then_inc(sQL, 16)
                                cnt["QL"] += 16
                                s.wait_ge(sQL, cnt["QL"])

                @blk.tensor
                def _(t):
                    for it in range(NITD):
                        p = it % 2
                        t.wait_ge(sDW[p], 16 * (it // 2 + 1))
                        for t8 in range(8):
                            j = it * 8 + t8
                            t.wait_ge(sDH, (j + 1) * 16)
                            if j >= 2:
                                t.wait_ge(sDV2, j - 1)
                            last = None
                            for fc in range(32):
                                last = t.matmul(
                                    psd[j % 2][:, :],
                                    hls[j % 2][:, fc * 128:(fc + 1) * 128],
                                    dts[p][:, fc * 512:(fc + 1) * 512],
                                    start=(fc == 0), stop=(fc == 31))
                            last.then_inc(sPE2, 1)

                @blk.vector
                def _(v):
                    for it in range(NITD):
                        e = it % E
                        tcb = (it // E) % 4
                        for t8 in range(8):
                            j = it * 8 + t8
                            t128 = tcb * 8 + t8
                            v.wait_ge(sPE2, j + 1)
                            if e == 0 and it >= 8 and t8 == 0:
                                v.wait_ge(sSI, it // 8)
                            cslice = combine[:, t128 * 8 + e:t128 * 8 + e + 1]
                            if e == 0:
                                v.tensor_scalar(
                                    acc[:, t8 * 512:(t8 + 1) * 512],
                                    psd[j % 2][:, :], cslice, None,
                                    ALU.mult).then_inc(sDV2, 1)
                            else:
                                v.scalar_tensor_tensor(
                                    acc[:, t8 * 512:(t8 + 1) * 512],
                                    psd[j % 2][:, :], cslice,
                                    acc[:, t8 * 512:(t8 + 1) * 512],
                                    ALU.mult, ALU.add).then_inc(sDV2, 1)

                @blk.scalar
                def _(a):
                    nsil = 0
                    nsj = 0
                    for it in range(NITD):
                        hh = it // (4 * E)
                        tcb = (it // E) % 4
                        e = it % E
                        if e != E - 1:
                            continue
                        a.wait_ge(sDV2, (it + 1) * 8)
                        lastsil = None
                        for t8 in range(8):
                            lastsil = act(a, gmsb[:, t8 * 512:(t8 + 1) * 512],
                                          acc[:, t8 * 512:(t8 + 1) * 512],
                                          AF.Silu)
                        lastsil.then_inc(sSJ, 1)
                        nsj += 1
                        a.wait_ge(sSJ, nsj)
                        for t8 in range(8):
                            t128 = tcb * 8 + t8
                            a.dma_start(
                                gmoe_d[t128, :, hh * 512:(hh + 1) * 512],
                                gmsb[:, t8 * 512:(t8 + 1) * 512]
                            ).then_inc(sQS, 16)
                            cnt["QS"] += 16
                        a.wait_ge(sQS, cnt["QS"])
                        if dbg and it == 7:
                            a.dma_start(dbg_acc0[:, :], acc[:, :]
                                        ).then_inc(sQS, 16)
                            cnt["QS"] += 16
                            a.wait_ge(sQS, cnt["QS"])
                        nsil += 1
                        a.sem_inc(sSI, 1)
                    if dbg:
                        a.dma_start(dbg_acc[:, :], acc[:, :]).then_inc(sQS, 16)
                        cnt["QS"] += 16
                        a.wait_ge(sQS, cnt["QS"])

        with ExitStack() as bs:
            def BSB(name, shape, dt):
                return bs.enter_context(nc.sbuf_tensor(name, shape, dt))
            w1_sb = BSB("w1_sb", [128, 8 * FF], BF16)
            w2_sb = BSB("w2_sb", [128, 32 * D], BF16)
            b1c = BSB("b1c_sb", [128, 32], F32)
            b2b = BSB("b2b_sb", [128, D], BF16)
            l1sb = BSB("l1sb_sb", [128, D], BF16)
            l1bb = BSB("l1bb_sb", [128, D], BF16)
            l2sb = BSB("l2sb_sb", [128, D], BF16)
            l2bb = BSB("l2bb_sb", [128, D], BF16)
            xb_tm = BSB("xb_tm", [128, 2 * D], F32)
            wol = BSB("wol", [128, 2 * D], F32)
            gml = BSB("gml", [128, 2 * D], BF16)
            t1b = BSB("t1b", [128, 2 * D], F32)
            xh_tm = BSB("xh_tm", [128, 2 * D], F32)
            xh_bf = BSB("xh_bf", [128, 2 * D], BF16)
            xh_fm = BSB("xh_fm", [128, 8 * TTB], BF16)
            hh_fm = BSB("hh_fm", [128, 32 * TTB], BF16)
            stb = BSB("stb", [128, 32], F32)
            ps_big = bs.enter_context(nc.psum_tensor("ps_big", [128, 2 * D], F32))
            trb_ps = bs.enter_context(nc.psum_tensor("trb_ps", [128, 2 * D], BF16))
            p1 = [bs.enter_context(nc.psum_tensor(f"p1_{i}", [128, TTB], F32))
                  for i in range(2)]

            with nc.Block() as blk:
                @blk.gpsimd
                def _(g):
                    g.dma_start(w1_sb.ap().rearrange("p (c m) -> p c m", c=8),
                                w1_d.ap().rearrange("(c p) m -> p c m", p=128)
                                ).then_inc(sQC, 16)
                    g.dma_start(w2_sb.ap().rearrange("p (c m) -> p c m", c=32),
                                w2_d.ap().rearrange("(c p) m -> p c m", p=128)
                                ).then_inc(sQC, 16)
                    g.dma_start(b1c[:, :], b1c_d[:, :]).then_inc(sQC, 16)
                    g.dma_start(b2b[:, :], b2b_d[:, :]).then_inc(sQC, 16)
                    g.dma_start(l1sb[:, :], l1sb_d[:, :]).then_inc(sQC, 16)
                    g.dma_start(l1bb[:, :], l1bb_d[:, :]).then_inc(sQC, 16)
                    g.dma_start(l2sb[:, :], l2sb_d[:, :]).then_inc(sQC, 16)
                    g.dma_start(l2bb[:, :], l2bb_d[:, :]).then_inc(sQC, 16)
                    cnt["QC"] += 128
                    g.wait_ge(sQC, cnt["QC"])

            sched = []

            def emit_ln_tm(sched, src, dst, eps_ap, scale_t, bias_t):
                nsub = 2

                def f1(v, src=src):
                    return v.reduce_sum(
                        stb[:, 0:2],
                        src.ap().rearrange("p (s d) -> p s d", s=nsub), axis=AX)
                sched.append(("v", f1))

                def f2(a, src=src, dst=dst):
                    last = None
                    for s in range(nsub):
                        last = act(a, dst[:, s * D:(s + 1) * D],
                                            src[:, s * D:(s + 1) * D],
                                            AF.Square,
                                            accum_out=stb[:, 4 + s:5 + s])
                    return last
                sched.append(("a", f2))

                def f3(v):
                    v.tensor_scalar(stb[:, 2:4], stb[:, 0:2], 1.0 / D, None,
                                    ALU.mult)
                    v.tensor_scalar(stb[:, 6:8], stb[:, 4:6], 1.0 / D, None,
                                    ALU.mult)
                    v.tensor_mul(stb[:, 8:10], stb[:, 2:4], stb[:, 2:4])
                    v.memset(stb[:, 30:31], 0.0)
                    return v.tensor_sub(stb[:, 8:10], stb[:, 6:8], stb[:, 8:10])
                sched.append(("v", f3))

                def f4(a, eps_ap=eps_ap):
                    return act(a, stb[:, 10:12], stb[:, 8:10], AF.Sqrt,
                                        bias=eps_ap)
                sched.append(("a", f4))

                def f5(v, src=src, dst=dst, scale_t=scale_t, bias_t=bias_t):
                    v.reciprocal(stb[:, 12:14], stb[:, 10:12])
                    v.memset(stb[:, 30:31], 0.0)
                    last = None
                    for s in range(nsub):
                        last = v.tensor_scalar(dst[:, s * D:(s + 1) * D],
                                               src[:, s * D:(s + 1) * D],
                                               stb[:, 2 + s:3 + s],
                                               stb[:, 12 + s:13 + s],
                                               ALU.subtract, ALU.mult)
                    if scale_t is not None:
                        for s in range(nsub):
                            v.tensor_mul(dst[:, s * D:(s + 1) * D],
                                         dst[:, s * D:(s + 1) * D],
                                         scale_t[:, :])
                            last = v.tensor_add(dst[:, s * D:(s + 1) * D],
                                                dst[:, s * D:(s + 1) * D],
                                                bias_t[:, :])
                    return last
                sched.append(("v", f5))

            def emit_b(ttb):
                t0 = ttb * TTB

                def f_l(s, ttb=ttb, t0=t0):
                    s.dma_start(
                        xb_tm.ap().rearrange("p (s d) -> p s d", s=2),
                        x_d[t0:t0 + TTB, :].rearrange("(s p) d -> p s d", p=128)
                    ).then_inc(sQL, 16)
                    cnt["QL"] += 16
                    for s2 in range(2):
                        s.dma_start(wol[:, s2 * D:(s2 + 1) * D],
                                    wo_out_d[ttb * 2 + s2, :, :]
                                    ).then_inc(sQL, 16)
                        s.dma_start(gml[:, s2 * D:(s2 + 1) * D],
                                    gmoe_d[ttb * 2 + s2, :, :]
                                    ).then_inc(sQL, 16)
                        cnt["QL"] += 32
                    s.wait_ge(sQL, cnt["QL"])
                    return None
                sched.append(("s", f_l))

                emit_ln_tm(sched, xb_tm, t1b, epsc[:, 0:1], None, None)

                def f_t1(v):
                    return v.tensor_add(t1b[:, :], t1b[:, :], wol[:, :])
                sched.append(("v", f_t1))

                emit_ln_tm(sched, t1b, xh_tm, epsc[:, 1:2], l1sb, l1bb)

                def f_cp(v):
                    return v.tensor_copy(xh_bf[:, :], xh_tm[:, :])
                sched.append(("v", f_cp))

                def f_tr(t):
                    last = None
                    for s in range(2):
                        for dc in range(DCN):
                            last = t.transpose(
                                trb_ps[:, dc * TTB + s * 128:
                                       dc * TTB + (s + 1) * 128],
                                xh_bf[:, s * D + dc * 128:s * D + (dc + 1) * 128],
                                ident[:, :])
                    return last
                sched.append(("t", f_tr))

                def f_tre(v):
                    return v.tensor_copy(xh_fm[:, :], trb_ps[:, :])
                sched.append(("v", f_tre))

                for fcp in range(16):
                    def f_w1(t, fcp=fcp):
                        last = None
                        for i in range(2):
                            fc = fcp * 2 + i
                            for kc in range(8):
                                last = t.matmul(
                                    p1[i][:, :],
                                    w1_sb[:, kc * FF + fc * 128:
                                          kc * FF + (fc + 1) * 128],
                                    xh_fm[:, kc * TTB:(kc + 1) * TTB],
                                    start=(kc == 0), stop=(kc == 7))
                        return last
                    sched.append(("t", f_w1))

                    def f_g1(a, fcp=fcp):
                        last = None
                        for i in range(2):
                            fc = fcp * 2 + i
                            last = act(a, 
                                hh_fm[:, fc * TTB:(fc + 1) * TTB],
                                p1[i][:, :], AF.Gelu_apprx_tanh,
                                bias=b1c[:, fc:fc + 1])
                        return last
                    sched.append(("a", f_g1))

                for s in range(2):
                    def f_w2(t, s=s):
                        last = None
                        for half in range(2):
                            for fc in range(32):
                                last = t.matmul(
                                    ps_big[:, s * D + half * 512:
                                           s * D + (half + 1) * 512],
                                    hh_fm[:, fc * TTB + s * 128:
                                          fc * TTB + (s + 1) * 128],
                                    w2_sb[:, fc * D + half * 512:
                                          fc * D + (half + 1) * 512],
                                    start=(fc == 0), stop=(fc == 31))
                        return last
                    sched.append(("t", f_w2))

                def f_res(v):
                    for s in range(2):
                        v.scalar_tensor_tensor(
                            wol[:, s * D:(s + 1) * D],
                            ps_big[:, s * D:(s + 1) * D], 1.0, b2b[:, :],
                            ALU.mult, ALU.add)
                    return v.tensor_add(wol[:, :], wol[:, :], xh_tm[:, :])
                sched.append(("v", f_res))

                emit_ln_tm(sched, wol, t1b, epsc[:, 1:2], l2sb, l2bb)

                def f_fin(v):
                    v.tensor_mul(t1b[:, :], t1b[:, :], gml[:, :])
                    return v.tensor_add(wol[:, :], xb_tm[:, :], t1b[:, :])
                sched.append(("v", f_fin))

                def f_out(g, t0=t0):
                    g.dma_start(
                        y_d[t0:t0 + TTB, :].rearrange("(s p) d -> p s d", p=128),
                        wol.ap().rearrange("p (s d) -> p s d", s=2)
                    ).then_inc(sQO, 16)
                    cnt["QO"] += 16
                    return None
                sched.append(("g", f_out))

            for ttb in range(NTTB):
                emit_b(ttb)
            _sched_run(nc, sB, sched)

        if dbg:
            with nc.Block() as blk:
                @blk.gpsimd
                def _(g):
                    g.dma_start(dbg_comb[:, :], combine[:, :]).then_inc(sQS, 16)
                    cnt["QS"] += 16
                    g.dma_start(dbg_stash.ap(), stash_d.ap()).then_inc(sQS, 16)
                    g.dma_start(dbg_z.ap(), z_d.ap()).then_inc(sQS, 16)
                    g.dma_start(dbg_summ.ap(), summ_ag_d.ap()).then_inc(sQS, 16)
                    g.dma_start(dbg_crow.ap(), crow_d.ap()).then_inc(sQS, 16)
                    g.dma_start(dbg_wo.ap(), wo_out_d.ap()).then_inc(sQS, 16)
                    g.dma_start(dbg_gm.ap(), gmoe_d.ap()).then_inc(sQS, 16)
                    g.dma_start(dbg_hact.ap(),
                                hact_d.ap()[0:2, :, :, 0:512]).then_inc(sQS, 16)
                    cnt["QS"] += 112
                    g.wait_ge(sQS, cnt["QS"])

        with nc.Block() as blk:
            @blk.sync
            def _(s):
                s.wait_ge(sQO, cnt["QO"])

        for k, vv in cnt.items():
            assert vv < 60000, (k, vv)
    return nc


def kernel(**inputs):
    x = np.asarray(inputs["L0_attn_x"])
    bf = ml_dtypes.bfloat16

    def f32c(a):
        return np.ascontiguousarray(np.asarray(a), dtype=np.float32)

    wq = f32c(inputs["L0_attn_Wq"]); wk = f32c(inputs["L0_attn_Wk"])
    wv = f32c(inputs["L0_attn_Wv"]); wa = f32c(inputs["L0_attn_Wa"])
    wg = f32c(inputs["L0_attn_Wg"]); wo = f32c(inputs["L0_attn_Wo"])
    w1 = f32c(inputs["L0_attn_W1"]); w2 = f32c(inputs["L0_attn_W2"])
    b1 = f32c(inputs["L0_attn_b1"]); b2 = f32c(inputs["L0_attn_b2"])
    l1s = f32c(inputs["L0_attn_ln1_scale"]); l1b = f32c(inputs["L0_attn_ln1_bias"])
    l2s = f32c(inputs["L0_attn_ln2_scale"]); l2b = f32c(inputs["L0_attn_ln2_bias"])
    gns = f32c(inputs["L0_attn_gn_scale"]); gnb = f32c(inputs["L0_attn_gn_bias"])
    gate_w = f32c(inputs["L0_ffn_gate_w"])
    gu = f32c(inputs["L0_ffn_gate_up_w"])
    dw = f32c(inputs["L0_ffn_down_w"])

    guT = np.ascontiguousarray(gu.transpose(0, 2, 1))
    dwT = np.ascontiguousarray(dw.transpose(0, 2, 1))
    gwT = np.ascontiguousarray(gate_w.T)
    b1c = np.ascontiguousarray(b1.reshape(32, 128).T)
    b2b = np.ascontiguousarray(np.broadcast_to(b2, (128, D)))
    l1sb = np.ascontiguousarray(np.broadcast_to(l1s, (128, D)))
    l1bb = np.ascontiguousarray(np.broadcast_to(l1b, (128, D)))
    l2sb = np.ascontiguousarray(np.broadcast_to(l2s, (128, D)))
    l2bb = np.ascontiguousarray(np.broadcast_to(l2b, (128, D)))
    gnsr = gns.reshape(1, D)
    gnbr = gnb.reshape(1, D)
    ident = np.eye(128, dtype=bf)
    gmap = np.zeros((128, 4), np.float32)
    for p in range(128):
        gmap[p, p // 32] = 1.0

    in_maps = []
    for c in range(8):
        b = c // 4
        t0 = (c % 4) * T
        onehot = np.zeros((128, 8), np.float32)
        onehot[:, c] = 1.0
        bsel = np.zeros((128, 2), np.float32)
        bsel[:, b] = 1.0
        m = {
            "x": np.ascontiguousarray(x[b, t0:t0 + T, :], dtype=np.float32),
            "wq": wq, "wk": wk, "wv": wv, "wa": wa, "wg": wg, "wo": wo,
            "w1": w1, "w2": w2, "b1c": b1c, "b2b": b2b,
            "l1sb": l1sb, "l1bb": l1bb, "l2sb": l2sb, "l2bb": l2bb,
            "gnsr": gnsr, "gnbr": gnbr, "gwT": gwT, "guT": guT, "dwT": dwT,
            "ident": ident, "gmap": gmap, "onehot": onehot, "bsel": bsel,
        }
        in_maps.append(m)

    dbg = bool(inputs.get("_dbg", False))
    nc = build_nc(dbg=dbg)
    res = run_bass_kernel_spmd(nc, in_maps, core_ids=list(range(8)))
    out = np.empty((2, 16384, D), np.float32)
    for c in range(8):
        b = c // 4
        t0 = (c % 4) * T
        out[b, t0:t0 + T, :] = res.results[c]["y"]
    if dbg:
        return out, res
    return out


if __name__ == "__main__":
    print("building...")
    nc = build_nc()
    print("built ok; instructions:", nc.n_instructions())


# version 02
# speedup vs baseline: 1.0077x; 1.0077x over previous
import numpy as np
import ml_dtypes
from contextlib import ExitStack

import concourse.bass as bass
import concourse.mybir as mybir
from concourse.bass_utils import run_bass_kernel_spmd

F32 = mybir.dt.float32
BF16 = mybir.dt.bfloat16
AF = mybir.ActivationFunctionType
ALU = mybir.AluOpType
AX = mybir.AxisListType.X

PI = float(np.pi)
MAGIC = 12582912.0
INV2PI = float(1.0 / (2.0 * np.pi))
TWOPI = float(2.0 * np.pi)

D = 1024
FF = 4096
E = 8
T = 4096
TTA = 512
NTT = T // TTA
TTB = 256
NTTB = T // TTB
DCN = 8
SUB = 64
NSUB = TTA // SUB
CNT_INV = float(1.0 / (16384.0 * 32.0))


def _sched_run(nc, sem, sched):
    def body(eng_name):
        def _(eng):
            for i, (en, fn) in enumerate(sched):
                if en != eng_name:
                    continue
                eng.wait_ge(sem, i)
                last = fn(eng)
                if last is not None:
                    last.then_inc(sem, 1)
                else:
                    eng.sem_inc(sem, 1)
        return _
    with nc.Block() as blk:
        blk.sync(body("s"))
        blk.tensor(body("t"))
        blk.vector(body("v"))
        blk.scalar(body("a"))
        blk.gpsimd(body("g"))


def _act_setids():
    from concourse.hw_specs import get_activation_tables
    tabs = list(get_activation_tables("gen3").values())
    def find(*funcs):
        for i, fs in enumerate(tabs):
            if all(f in fs for f in funcs):
                return i
        raise ValueError(funcs)
    return {
        AF.Sqrt: find(AF.Sqrt, AF.Square, AF.Sign),
        AF.Sigmoid: find(AF.Sigmoid, AF.Arctan, AF.Sign),
        AF.Arctan: find(AF.Sigmoid, AF.Arctan, AF.Sign),
        AF.Ln: find(AF.Ln, AF.Exp),
        AF.Exp: find(AF.Ln, AF.Exp),
        AF.Sin: find(AF.Sin, AF.Silu),
        AF.Silu: find(AF.Sin, AF.Silu),
        AF.Gelu: find(AF.Gelu),
        AF.Gelu_apprx_tanh: find(AF.Gelu_apprx_tanh),
    }


_ACT_WILD = {AF.Square, AF.Sign, AF.Abs, AF.Copy, AF.Identity, AF.Relu,
             AF.Is_finite}


def build_nc(dbg=False):
    nc = bass.Bass("TRN2", target_bir_lowering=False)
    _setid = _act_setids()
    _att = {"cur": None}

    def _tload(a, sid):
        a.add_instruction(mybir.InstLoadActFuncSet(
            name=nc.get_next_instruction_name(),
            act_func_set_id=sid, ins=[], outs=[]))
        _att["cur"] = sid

    def act(a, out, in_, func, **kw):
        if func in _ACT_WILD:
            if _att["cur"] is None:
                _tload(a, _setid[AF.Sqrt])
        elif _setid[func] != _att["cur"]:
            _tload(a, _setid[func])
        return a.activation(out, in_, func, **kw)

    x_d = nc.dram_tensor("x", [T, D], F32, kind="ExternalInput")
    wq_d = nc.dram_tensor("wq", [D, D], F32, kind="ExternalInput")
    wk_d = nc.dram_tensor("wk", [D, D], F32, kind="ExternalInput")
    wv_d = nc.dram_tensor("wv", [D, D], F32, kind="ExternalInput")
    wa_d = nc.dram_tensor("wa", [D, 2 * D], F32, kind="ExternalInput")
    wg_d = nc.dram_tensor("wg", [D, D], F32, kind="ExternalInput")
    wo_d = nc.dram_tensor("wo", [D, D], F32, kind="ExternalInput")
    w1_d = nc.dram_tensor("w1", [D, FF], F32, kind="ExternalInput")
    w2_d = nc.dram_tensor("w2", [FF, D], F32, kind="ExternalInput")
    b1c_d = nc.dram_tensor("b1c", [128, 32], F32, kind="ExternalInput")
    b2b_d = nc.dram_tensor("b2b", [128, D], F32, kind="ExternalInput")
    l1sb_d = nc.dram_tensor("l1sb", [128, D], F32, kind="ExternalInput")
    l1bb_d = nc.dram_tensor("l1bb", [128, D], F32, kind="ExternalInput")
    l2sb_d = nc.dram_tensor("l2sb", [128, D], F32, kind="ExternalInput")
    l2bb_d = nc.dram_tensor("l2bb", [128, D], F32, kind="ExternalInput")
    gnsr_d = nc.dram_tensor("gnsr", [1, D], F32, kind="ExternalInput")
    gnbr_d = nc.dram_tensor("gnbr", [1, D], F32, kind="ExternalInput")
    gwT_d = nc.dram_tensor("gwT", [D, E], F32, kind="ExternalInput")
    guT_d = nc.dram_tensor("guT", [E, D, 2 * FF], F32, kind="ExternalInput")
    dwT_d = nc.dram_tensor("dwT", [E, FF, D], F32, kind="ExternalInput")
    ident_d = nc.dram_tensor("ident", [128, 128], BF16, kind="ExternalInput")
    gmap_d = nc.dram_tensor("gmap", [128, 4], F32, kind="ExternalInput")
    onehot_d = nc.dram_tensor("onehot", [128, 8], F32, kind="ExternalInput")
    bsel_d = nc.dram_tensor("bsel", [128, 2], F32, kind="ExternalInput")

    y_d = nc.dram_tensor("y", [T, D], F32, kind="ExternalOutput")

    stash_d = nc.dram_tensor("stash", [6, DCN, 128, T], F32)
    z_d = nc.dram_tensor("zst", [DCN, 128, T], BF16)
    hact_d = nc.dram_tensor("hact", [E, 32, 128, T], BF16)
    gmoe_d = nc.dram_tensor("gmoe", [32, 128, D], BF16)
    wo_out_d = nc.dram_tensor("wo_out", [32, 128, D], F32)
    summ_in_d = nc.dram_tensor("summ_in", [128, 32], F32)
    summ_ag_d = nc.dram_tensor("summ_ag", [1024, 32], F32)
    stats_in_d = nc.dram_tensor("stats_in", [2, 128], F32)
    stats_out_d = nc.dram_tensor("stats_out", [2, 128], F32)
    crow_d = nc.dram_tensor("crow", [2, D], F32)

    if dbg:
        dbg_stash = nc.dram_tensor("dbg_stash", [6, DCN, 128, T], F32,
                                   kind="ExternalOutput")
        dbg_z = nc.dram_tensor("dbg_z", [DCN, 128, T], BF16,
                               kind="ExternalOutput")
        dbg_xnf = nc.dram_tensor("dbg_xnf", [128, DCN * T], BF16,
                                 kind="ExternalOutput")
        dbg_summ = nc.dram_tensor("dbg_summ", [1024, 32], F32,
                                  kind="ExternalOutput")
        dbg_crow = nc.dram_tensor("dbg_crow", [2, D], F32,
                                  kind="ExternalOutput")
        dbg_wo = nc.dram_tensor("dbg_wo", [32, 128, D], F32,
                                kind="ExternalOutput")
        dbg_gm = nc.dram_tensor("dbg_gm", [32, 128, D], BF16,
                                kind="ExternalOutput")
        dbg_hact = nc.dram_tensor("dbg_hact", [2, 32, 128, 512], BF16,
                                  kind="ExternalOutput")
        dbg_st4 = nc.dram_tensor("dbg_st4", [128, 8 * 24], F32,
                                 kind="ExternalOutput")
        dbg_comb = nc.dram_tensor("dbg_comb", [128, 256], F32,
                                  kind="ExternalOutput")
        dbg_acc = nc.dram_tensor("dbg_acc", [128, 8 * 512], F32,
                                 kind="ExternalOutput")
        dbg_acc0 = nc.dram_tensor("dbg_acc0", [128, 8 * 512], F32,
                                  kind="ExternalOutput")
        dbg_hls = nc.dram_tensor("dbg_hls", [128, 32 * 128], BF16,
                                 kind="ExternalOutput")

    es = ExitStack()
    with es:
        def SEM(name):
            return es.enter_context(nc.semaphore(name))
        sQL = SEM("sQL")
        sQC = SEM("sQC")
        sQS = SEM("sQS")
        sQO = SEM("sQO")
        sCC = SEM("sCC")
        sA1 = SEM("sA1")
        sQG = SEM("sQG")
        sA2 = SEM("sA2")
        sWO = SEM("sWO")
        sB = SEM("sB")
        sGW = [SEM("sGW0"), SEM("sGW1")]
        sPE1 = SEM("sPE1")
        sAC1 = SEM("sAC1")
        sDV1 = SEM("sDV1")
        sHAb = [SEM(f"sHA{i}") for i in range(4)]
        sDW = [SEM("sDW0"), SEM("sDW1")]
        sDH = SEM("sDH")
        sPE2 = SEM("sPE2")
        sDV2 = SEM("sDV2")
        sSI = SEM("sSI")
        sSJ = SEM("sSJ")
        cnt = {"QL": 0, "QC": 0, "QS": 0, "QO": 0, "CC": 0}

        def SB(name, shape, dt):
            return es.enter_context(nc.sbuf_tensor(name, shape, dt))

        mask64 = SB("mask64", [128, TTA], F32)
        ident = SB("identsb", [128, 128], BF16)
        gmap = SB("gmapsb", [128, 4], F32)
        onehot = SB("onehotsb", [128, 8], F32)
        bsel = SB("bselsb", [128, 2], F32)
        combine = SB("combine", [128, 32 * E], F32)
        summ = SB("summ", [128, 32], F32)
        hg = SB("hg", [128, 16], F32)
        stats_acc = SB("stats_acc", [128, 32], F32)
        c1pp = SB("c1pp", [128, 8], F32)
        c2pp = SB("c2pp", [128, 8], F32)
        epsc = SB("epsc", [128, 4], F32)
        st4c_ref = SB("st4c", [128, 8 * 24], F32)

        with nc.Block() as blk:
            @blk.gpsimd
            def _(g):
                g.dma_start(ident[:, :], ident_d[:, :]).then_inc(sQC, 16)
                g.dma_start(gmap[:, :], gmap_d[:, :]).then_inc(sQC, 16)
                g.dma_start(onehot[:, :], onehot_d[:, :]).then_inc(sQC, 16)
                g.dma_start(bsel[:, :], bsel_d[:, :]).then_inc(sQC, 16)
                cnt["QC"] += 64
                g.wait_ge(sQC, cnt["QC"])

            @blk.vector
            def _(v):
                v.memset(mask64[:, :], 1.0)
                v.memset(mask64.ap()[:, 0:TTA:SUB], 0.0)
                v.memset(summ[:, :], 0.0)
                v.memset(summ[:, 0:8], 1.0)
                v.memset(stats_acc[:, :], 0.0)
                v.memset(epsc[:, 0:1], 1e-5)
                v.memset(epsc[:, 1:2], 1e-6)
                v.memset(epsc[:, 2:3], 1e-30)

        xnscope = ExitStack()
        xnf = xnscope.enter_context(nc.sbuf_tensor("xnf", [128, DCN * T], BF16))

        with ExitStack() as a1s:
            def A1SB(name, shape, dt):
                return a1s.enter_context(nc.sbuf_tensor(name, shape, dt))
            wk_sb = A1SB("wk_sb", [128, 8 * D], BF16)
            wv_sb = A1SB("wv_sb", [128, 8 * D], BF16)
            wa_sb = A1SB("wa_sb", [128, 8 * 2 * D], BF16)
            x_tm = A1SB("x_tm", [128, 4 * D], F32)
            xn_tmb = A1SB("xn_tmb", [128, 4 * D], BF16)
            ln_scr = A1SB("ln_scr", [128, D], F32)
            st4 = A1SB("st4", [128, 24], F32)
            b_are = A1SB("b_are", [128, TTA], F32)
            b_aim = A1SB("b_aim", [128, TTA], F32)
            b_kv = A1SB("b_kv", [128, TTA], F32)
            U = [A1SB(f"u{i}", [128, TTA], F32) for i in range(13)]
            csm = A1SB("csm", [128, 120], F32)
            pk = a1s.enter_context(nc.psum_tensor("pk", [128, TTA], F32))
            pv = a1s.enter_context(nc.psum_tensor("pv", [128, TTA], F32))
            par_ps = a1s.enter_context(nc.psum_tensor("par", [128, TTA], F32))
            pai = a1s.enter_context(nc.psum_tensor("pai", [128, TTA], F32))
            tr_ps = a1s.enter_context(nc.psum_tensor("tr_ps", [128, 4096], BF16))

            with nc.Block() as blk:
                @blk.gpsimd
                def _(g):
                    g.dma_start(wk_sb.ap().rearrange("p (c m) -> p c m", c=8),
                                wk_d.ap().rearrange("(c p) m -> p c m", p=128)
                                ).then_inc(sQC, 16)
                    g.dma_start(wv_sb.ap().rearrange("p (c m) -> p c m", c=8),
                                wv_d.ap().rearrange("(c p) m -> p c m", p=128)
                                ).then_inc(sQC, 16)
                    g.dma_start(wa_sb.ap().rearrange("p (c m) -> p c m", c=8),
                                wa_d.ap().rearrange("(c p) m -> p c m", p=128)
                                ).then_inc(sQC, 16)
                    cnt["QC"] += 48
                    g.wait_ge(sQC, cnt["QC"])

            sched = []

            def emit_tt(tt):
                t0 = tt * TTA

                def f_load(eng, t0=t0):
                    eng.dma_start(
                        x_tm.ap().rearrange("p (s d) -> p s d", s=4),
                        x_d[t0:t0 + TTA, :].rearrange("(s p) d -> p s d", p=128)
                    ).then_inc(sQL, 16)
                    cnt["QL"] += 16
                    eng.wait_ge(sQL, cnt["QL"])
                    return None
                sched.append(("s", f_load))

                def f_ln1(v):
                    return v.reduce_sum(
                        st4[:, 0:4],
                        x_tm.ap().rearrange("p (s d) -> p s d", s=4), axis=AX)
                sched.append(("v", f_ln1))

                def f_ln2(a):
                    last = None
                    for s in range(4):
                        last = act(a, ln_scr[:, :], x_tm[:, s * D:(s + 1) * D],
                                            AF.Square,
                                            accum_out=st4[:, 8 + s:9 + s])
                    return last
                sched.append(("a", f_ln2))

                def f_ln3(v):
                    v.tensor_scalar(st4[:, 4:8], st4[:, 0:4], 1.0 / D, None,
                                    ALU.mult)
                    v.tensor_scalar(st4[:, 12:16], st4[:, 8:12], 1.0 / D, None,
                                    ALU.mult)
                    v.tensor_mul(st4[:, 16:20], st4[:, 4:8], st4[:, 4:8])
                    v.memset(st4[:, 0:1], 0.0)
                    return v.tensor_sub(st4[:, 16:20], st4[:, 12:16],
                                        st4[:, 16:20])
                sched.append(("v", f_ln3))

                def f_ln4(a):
                    return act(a, st4[:, 20:24], st4[:, 16:20], AF.Sqrt,
                                        bias=epsc[:, 0:1])
                sched.append(("a", f_ln4))

                def f_ln5(v, tt=tt):
                    v.reciprocal(st4[:, 16:20], st4[:, 20:24])
                    v.memset(st4[:, 0:1], 0.0)
                    for s in range(4):
                        v.tensor_scalar(
                            xn_tmb[:, s * D:(s + 1) * D],
                            x_tm[:, s * D:(s + 1) * D],
                            st4[:, 4 + s:5 + s], st4[:, 16 + s:17 + s],
                            ALU.subtract, ALU.mult)
                    return v.tensor_copy(st4c_ref[:, tt * 24:(tt + 1) * 24],
                                         st4[:, :])
                sched.append(("v", f_ln5))

                def f_tr(t):
                    last = None
                    for dc in range(DCN):
                        for s in range(4):
                            last = t.transpose(
                                tr_ps[:, dc * 512 + s * 128:dc * 512 + (s + 1) * 128],
                                xn_tmb[:, s * D + dc * 128:s * D + (dc + 1) * 128],
                                ident[:, :])
                    return last
                sched.append(("t", f_tr))

                def f_tre(v, tt=tt):
                    last = None
                    for dc in range(DCN):
                        last = v.tensor_copy(
                            xnf[:, dc * T + tt * TTA:dc * T + (tt + 1) * TTA],
                            tr_ps[:, dc * 512:(dc + 1) * 512])
                    return last
                sched.append(("v", f_tre))

                for dc in range(DCN):
                    def f_pe(t, tt=tt, dc=dc):
                        last = None
                        for (ps, wsb, wwid, col0) in (
                                (pk, wk_sb, D, dc * 128),
                                (pv, wv_sb, D, dc * 128),
                                (par_ps, wa_sb, 2 * D, dc * 128),
                                (pai, wa_sb, 2 * D, D + dc * 128)):
                            for kc in range(8):
                                last = t.matmul(
                                    ps[:, :],
                                    wsb[:, kc * wwid + col0:kc * wwid + col0 + 128],
                                    xnf[:, kc * T + tt * TTA:
                                        kc * T + (tt + 1) * TTA],
                                    start=(kc == 0), stop=(kc == 7))
                        return last
                    sched.append(("t", f_pe))

                    def f_d1(v):
                        v.tensor_copy(U[10][:, :], pk[:, :])
                        v.tensor_mul(b_kv[:, :], U[10][:, :], pv[:, :])
                        v.tensor_copy(b_are[:, :], par_ps[:, :])
                        return v.tensor_copy(b_aim[:, :], pai[:, :])
                    sched.append(("v", f_d1))

                    def f_a1(a):
                        act(a, U[1][:, :], b_are[:, :], AF.Sign)
                        act(a, U[2][:, :], b_aim[:, :], AF.Sign)
                        act(a, U[5][:, :], b_are[:, :], AF.Square)
                        act(a, U[6][:, :], b_aim[:, :], AF.Square)
                        return act(a, U[3][:, :], U[1][:, :], AF.Square)
                    sched.append(("a", f_a1))

                    def f_d2(v):
                        v.tensor_add(U[0][:, :], U[5][:, :], U[6][:, :])
                        v.tensor_scalar(U[3][:, :], U[3][:, :], -1e-37, 1e-37,
                                        ALU.mult, ALU.add)
                        v.tensor_add(U[1][:, :], b_are[:, :], U[3][:, :])
                        v.reciprocal(U[3][:, :], U[1][:, :])
                        return v.tensor_mul(U[3][:, :], b_aim[:, :], U[3][:, :])
                    sched.append(("v", f_d2))

                    def f_a2(a):
                        act(a, U[0][:, :], U[0][:, :], AF.Sqrt,
                                     bias=epsc[:, 2:3])
                        return act(a, U[4][:, :], U[1][:, :], AF.Sign)
                    sched.append(("a", f_a2))

                    def f_a3(a):
                        act(a, U[0][:, :], U[0][:, :], AF.Sigmoid)
                        return act(a, U[5][:, :], U[3][:, :], AF.Arctan)
                    sched.append(("a", f_a3))

                    def f_d3(v):
                        v.tensor_scalar(U[4][:, :], U[4][:, :], -PI / 2, PI / 2,
                                        ALU.mult, ALU.add)
                        v.tensor_mul(U[2][:, :], U[2][:, :], U[4][:, :])
                        return v.tensor_add(U[5][:, :], U[5][:, :], U[2][:, :])
                    sched.append(("v", f_d3))

                    def f_a4(a):
                        return act(a, U[0][:, :], U[0][:, :], AF.Ln)
                    sched.append(("a", f_a4))

                    def f_d4(v):
                        v.tensor_tensor_scan(U[1][:, :], mask64[:, :], U[0][:, :],
                                             0.0, ALU.mult, ALU.add)
                        v.tensor_tensor_scan(U[3][:, :], mask64[:, :], U[5][:, :],
                                             0.0, ALU.mult, ALU.add)
                        v.tensor_scalar(U[4][:, :], U[3][:, :], INV2PI, MAGIC,
                                        ALU.mult, ALU.add)
                        v.tensor_scalar(U[4][:, :], U[4][:, :], -MAGIC, None,
                                        ALU.add)
                        return v.scalar_tensor_tensor(U[4][:, :], U[4][:, :],
                                                      -TWOPI, U[3][:, :],
                                                      ALU.mult, ALU.add)
                    sched.append(("v", f_d4))

                    def f_a5(a):
                        act(a, U[6][:, :], U[4][:, :], AF.Abs)
                        act(a, U[7][:, :], U[1][:, :], AF.Exp, scale=-1.0)
                        return act(a, U[1][:, :], U[1][:, :], AF.Exp)
                    sched.append(("a", f_a5))

                    def f_d5(v):
                        return v.tensor_scalar(U[6][:, :], U[6][:, :], -1.0,
                                               PI / 2, ALU.mult, ALU.add)
                    sched.append(("v", f_d5))

                    def f_a6(a):
                        act(a, U[4][:, :], U[4][:, :], AF.Sin)
                        return act(a, U[6][:, :], U[6][:, :], AF.Sin)
                    sched.append(("a", f_a6))

                    def f_d6(v, dc=dc):
                        c = csm
                        v.tensor_mul(U[3][:, :], U[7][:, :], b_kv[:, :])
                        v.tensor_mul(U[8][:, :], U[3][:, :], U[6][:, :])
                        v.tensor_mul(U[9][:, :], U[3][:, :], U[4][:, :])
                        v.reduce_sum(c[:, 0:8],
                                     U[8].ap().rearrange("p (c j) -> p c j", c=8),
                                     axis=AX)
                        v.reduce_sum(c[:, 8:16],
                                     U[9].ap().rearrange("p (c j) -> p c j", c=8),
                                     axis=AX)
                        eL63 = U[1].ap()[:, SUB - 1:TTA:SUB]
                        cos63 = U[6].ap()[:, SUB - 1:TTA:SUB]
                        sin63 = U[4].ap()[:, SUB - 1:TTA:SUB]
                        v.tensor_mul(c[:, 16:24], eL63, cos63)
                        v.tensor_mul(c[:, 24:32], eL63, sin63)
                        v.tensor_copy(c[:, 48:49], summ[:, 16 + dc:17 + dc])
                        v.tensor_copy(c[:, 57:58], summ[:, 24 + dc:25 + dc])
                        v.tensor_copy(c[:, 66:67], summ[:, 0 + dc:1 + dc])
                        v.tensor_copy(c[:, 75:76], summ[:, 8 + dc:9 + dc])
                        v.tensor_mul(c[:, 84:92], c[:, 16:24], c[:, 0:8])
                        v.tensor_mul(c[:, 32:40], c[:, 24:32], c[:, 8:16])
                        v.tensor_mul(c[:, 96:104], c[:, 24:32], c[:, 0:8])
                        v.tensor_mul(c[:, 40:48], c[:, 16:24], c[:, 8:16])
                        v.tensor_add(c[:, 32:40], c[:, 84:92], c[:, 32:40])
                        v.tensor_sub(c[:, 40:48], c[:, 96:104], c[:, 40:48])
                        for cc in range(NSUB):
                            Aer = c[:, 16 + cc:17 + cc]
                            Aei = c[:, 24 + cc:25 + cc]
                            Hr = c[:, 48 + cc:49 + cc]
                            Hi = c[:, 57 + cc:58 + cc]
                            Pr = c[:, 66 + cc:67 + cc]
                            Pi = c[:, 75 + cc:76 + cc]
                            p1 = c[:, 104:105]; p2 = c[:, 105:106]
                            p3 = c[:, 106:107]; p4 = c[:, 107:108]
                            a1 = c[:, 108:109]; a2 = c[:, 109:110]
                            b1 = c[:, 110:111]; b2 = c[:, 111:112]
                            tA = c[:, 112:113]; tB = c[:, 113:114]
                            v.tensor_mul(p1, Pr, Aer)
                            v.tensor_mul(p2, Pi, Aei)
                            v.tensor_mul(p3, Pr, Aei)
                            v.tensor_mul(p4, Pi, Aer)
                            v.tensor_mul(a1, Aer, Hr)
                            v.tensor_mul(a2, Aei, Hi)
                            v.tensor_mul(b1, Aer, Hi)
                            v.tensor_mul(b2, Aei, Hr)
                            v.tensor_sub(c[:, 67 + cc:68 + cc], p1, p2)
                            v.tensor_add(c[:, 76 + cc:77 + cc], p3, p4)
                            v.tensor_sub(tA, a1, a2)
                            v.tensor_add(tB, b1, b2)
                            v.tensor_add(c[:, 49 + cc:50 + cc], tA,
                                         c[:, 32 + cc:33 + cc])
                            v.tensor_add(c[:, 58 + cc:59 + cc], tB,
                                         c[:, 40 + cc:41 + cc])
                        v.tensor_copy(summ[:, 0 + dc:1 + dc], c[:, 74:75])
                        v.tensor_copy(summ[:, 8 + dc:9 + dc], c[:, 83:84])
                        v.tensor_copy(summ[:, 16 + dc:17 + dc], c[:, 56:57])
                        v.tensor_copy(summ[:, 24 + dc:25 + dc], c[:, 65:66])
                        j0r = U[8].ap()[:, 0:TTA:SUB]
                        j0i = U[9].ap()[:, 0:TTA:SUB]
                        v.tensor_add(j0r, j0r, c[:, 48:56])
                        v.tensor_sub(j0i, j0i, c[:, 57:65])
                        v.tensor_tensor_scan(U[7][:, :], mask64[:, :], U[8][:, :],
                                             0.0, ALU.mult, ALU.add)
                        v.tensor_tensor_scan(U[3][:, :], mask64[:, :], U[9][:, :],
                                             0.0, ALU.mult, ALU.add)
                        v.tensor_mul(U[10][:, :], U[1][:, :], U[6][:, :])
                        v.tensor_mul(U[11][:, :], U[1][:, :], U[4][:, :])
                        v.tensor_mul(U[0][:, :], U[10][:, :], U[7][:, :])
                        v.tensor_mul(U[5][:, :], U[11][:, :], U[3][:, :])
                        v.tensor_add(U[0][:, :], U[0][:, :], U[5][:, :])
                        v.tensor_mul(U[5][:, :], U[11][:, :], U[7][:, :])
                        v.tensor_mul(U[6][:, :], U[10][:, :], U[3][:, :])
                        v.tensor_sub(U[5][:, :], U[5][:, :], U[6][:, :])
                        APrb = c.ap()[:, 66:74].unsqueeze(2).broadcast_to(
                            (128, 8, SUB))
                        APib = c.ap()[:, 75:83].unsqueeze(2).broadcast_to(
                            (128, 8, SUB))
                        Ar3 = U[10].ap().rearrange("p (c j) -> p c j", c=8)
                        Ai3 = U[11].ap().rearrange("p (c j) -> p c j", c=8)
                        t4v = U[4].ap().rearrange("p (c j) -> p c j", c=8)
                        t6v = U[6].ap().rearrange("p (c j) -> p c j", c=8)
                        t12v = U[12].ap().rearrange("p (c j) -> p c j", c=8)
                        v.tensor_tensor(t4v, Ar3, APrb, ALU.mult)
                        v.tensor_tensor(t6v, Ai3, APib, ALU.mult)
                        v.tensor_sub(U[4][:, :], U[4][:, :], U[6][:, :])
                        v.tensor_tensor(t6v, Ar3, APib, ALU.mult)
                        v.tensor_tensor(t12v, Ai3, APrb, ALU.mult)
                        return v.tensor_add(U[6][:, :], U[6][:, :], U[12][:, :])
                    sched.append(("v", f_d6))

                    def f_st(g, tt=tt, dc=dc):
                        t0 = tt * TTA
                        g.dma_start(stash_d[0, dc, :, t0:t0 + TTA],
                                    U[0][:, :]).then_inc(sQS, 16)
                        g.dma_start(stash_d[1, dc, :, t0:t0 + TTA],
                                    U[5][:, :]).then_inc(sQS, 16)
                        g.dma_start(stash_d[2, dc, :, t0:t0 + TTA],
                                    U[4][:, :]).then_inc(sQS, 16)
                        g.dma_start(stash_d[3, dc, :, t0:t0 + TTA],
                                    U[6][:, :]).then_inc(sQS, 16)
                        cnt["QS"] += 64
                        g.wait_ge(sQS, cnt["QS"])
                        return None
                    sched.append(("g", f_st))

            for tt in range(NTT):
                emit_tt(tt)
            _sched_run(nc, sA1, sched)

        with ExitStack() as x1s:
            agp = x1s.enter_context(nc.sbuf_tensor("agp", [128, 256], F32))
            prr = x1s.enter_context(nc.sbuf_tensor("prr", [128, 64], F32))
            pri = x1s.enter_context(nc.sbuf_tensor("pri", [128, 64], F32))
            mtmp = x1s.enter_context(nc.sbuf_tensor("mtmp", [128, 32], F32))
            selb = x1s.enter_context(nc.sbuf_tensor("selb", [128, 64], F32))

            with nc.Block() as blk:
                @blk.gpsimd
                def _(g):
                    g.dma_start(summ_in_d[:, :], summ[:, :]).then_inc(sQS, 16)
                    cnt["QS"] += 16
                    g.wait_ge(sQS, cnt["QS"])
                    g.collective_compute(
                        "AllGather", ALU.bypass,
                        replica_groups=[list(range(8))],
                        ins=[summ_in_d.ap().opt()],
                        outs=[summ_ag_d.ap().opt()]).then_inc(sCC, 1)
                    cnt["CC"] += 1
                    g.wait_ge(sCC, cnt["CC"])

            with nc.Block() as blk:
                @blk.sync
                def _(s):
                    s.dma_start(
                        agp.ap().rearrange("p (r c) -> p r c", r=8),
                        summ_ag_d.ap().rearrange("(r p) c -> p r c", p=128)
                    ).then_inc(sQL, 16)
                    cnt["QL"] += 16
                    s.wait_ge(sQL, cnt["QL"])

            with nc.Block() as blk:
                @blk.vector
                def _(v):
                    v.memset(prr[:, :], 0.0)
                    v.memset(pri[:, :], 0.0)
                    for r in range(1, 8):
                        if r == 4:
                            continue
                        co = (r - 1) * 32
                        def colA(dc):
                            return agp[:, co + dc:co + dc + 1]
                        def colAi(dc):
                            return agp[:, co + 8 + dc:co + 8 + dc + 1]
                        def colH(dc):
                            return agp[:, co + 16 + dc:co + 16 + dc + 1]
                        def colHi(dc):
                            return agp[:, co + 24 + dc:co + 24 + dc + 1]
                        for dc in range(DCN):
                            v.tensor_mul(mtmp[:, dc:dc + 1], colA(dc),
                                         prr[:, dc * 8 + r - 1:dc * 8 + r])
                        for dc in range(DCN):
                            v.tensor_mul(mtmp[:, 8 + dc:9 + dc], colAi(dc),
                                         pri[:, dc * 8 + r - 1:dc * 8 + r])
                        for dc in range(DCN):
                            v.tensor_mul(mtmp[:, 16 + dc:17 + dc], colA(dc),
                                         pri[:, dc * 8 + r - 1:dc * 8 + r])
                        for dc in range(DCN):
                            v.tensor_mul(mtmp[:, 24 + dc:25 + dc], colAi(dc),
                                         prr[:, dc * 8 + r - 1:dc * 8 + r])
                        for dc in range(DCN):
                            v.tensor_sub(mtmp[:, dc:dc + 1], mtmp[:, dc:dc + 1],
                                         mtmp[:, 8 + dc:9 + dc])
                        for dc in range(DCN):
                            v.tensor_add(mtmp[:, 16 + dc:17 + dc],
                                         mtmp[:, 16 + dc:17 + dc],
                                         mtmp[:, 24 + dc:25 + dc])
                        for dc in range(DCN):
                            v.tensor_add(prr[:, dc * 8 + r:dc * 8 + r + 1],
                                         mtmp[:, dc:dc + 1], colH(dc))
                        for dc in range(DCN):
                            v.tensor_add(pri[:, dc * 8 + r:dc * 8 + r + 1],
                                         mtmp[:, 16 + dc:17 + dc], colHi(dc))
                    for dc in range(DCN):
                        v.tensor_mul(selb[:, dc * 8:(dc + 1) * 8],
                                     prr[:, dc * 8:(dc + 1) * 8], onehot[:, :])
                    for dc in range(DCN):
                        v.reduce_sum(hg[:, 2 * dc:2 * dc + 1],
                                     selb[:, dc * 8:(dc + 1) * 8], axis=AX)
                    for dc in range(DCN):
                        v.tensor_mul(selb[:, dc * 8:(dc + 1) * 8],
                                     pri[:, dc * 8:(dc + 1) * 8], onehot[:, :])
                    for dc in range(DCN):
                        v.reduce_sum(hg[:, 2 * dc + 1:2 * dc + 2],
                                     selb[:, dc * 8:(dc + 1) * 8], axis=AX)

        with ExitStack() as qgs:
            def QGSB(name, shape, dt):
                return qgs.enter_context(nc.sbuf_tensor(name, shape, dt))
            wq_sb = QGSB("wq_sb", [128, 8 * D], BF16)
            wg_sb = QGSB("wg_sb", [128, 8 * D], BF16)
            gwt_sb = QGSB("gwt_sb", [128, 8 * E], BF16)
            qtile = QGSB("qtile", [128, TTA], F32)
            sgtile = QGSB("sgtile", [128, TTA], F32)
            rt = QGSB("rt", [128, 64], F32)
            pq = qgs.enter_context(nc.psum_tensor("pq", [128, TTA], F32))
            pg = qgs.enter_context(nc.psum_tensor("pg", [128, TTA], F32))
            pr = qgs.enter_context(nc.psum_tensor("pr", [128, 16], F32))

            with nc.Block() as blk:
                @blk.gpsimd
                def _(g):
                    g.dma_start(wq_sb.ap().rearrange("p (c m) -> p c m", c=8),
                                wq_d.ap().rearrange("(c p) m -> p c m", p=128)
                                ).then_inc(sQC, 16)
                    g.dma_start(wg_sb.ap().rearrange("p (c m) -> p c m", c=8),
                                wg_d.ap().rearrange("(c p) m -> p c m", p=128)
                                ).then_inc(sQC, 16)
                    g.dma_start(gwt_sb.ap().rearrange("p (c e) -> p c e", c=8),
                                gwT_d.ap().rearrange("(c p) e -> p c e", p=128)
                                ).then_inc(sQC, 16)
                    cnt["QC"] += 48
                    g.wait_ge(sQC, cnt["QC"])

            sched = []

            def emit_qg(tt):
                for dc in range(DCN):
                    def f_pe(t, tt=tt, dc=dc):
                        last = None
                        for (ps, wsb) in ((pq, wq_sb), (pg, wg_sb)):
                            for kc in range(8):
                                last = t.matmul(
                                    ps[:, :],
                                    wsb[:, kc * D + dc * 128:
                                        kc * D + dc * 128 + 128],
                                    xnf[:, kc * T + tt * TTA:
                                        kc * T + (tt + 1) * TTA],
                                    start=(kc == 0), stop=(kc == 7))
                        return last
                    sched.append(("t", f_pe))

                    def f_ev(v):
                        return v.tensor_copy(qtile[:, :], pq[:, :])
                    sched.append(("v", f_ev))

                    def f_si(a, dc=dc):
                        return act(a, sgtile[:, :], pg[:, :], AF.Silu)
                    sched.append(("a", f_si))

                    def f_st(g, tt=tt, dc=dc):
                        t0 = tt * TTA
                        g.dma_start(stash_d[4, dc, :, t0:t0 + TTA],
                                    qtile[:, :]).then_inc(sQS, 16)
                        g.dma_start(stash_d[5, dc, :, t0:t0 + TTA],
                                    sgtile[:, :]).then_inc(sQS, 16)
                        cnt["QS"] += 32
                        g.wait_ge(sQS, cnt["QS"])
                        return None
                    sched.append(("g", f_st))

                for sp in range(2):
                    def f_rpe(t, tt=tt, sp=sp):
                        last = None
                        for k in range(2):
                            s = sp * 2 + k
                            for kc in range(8):
                                last = t.matmul(
                                    pr[:, k * 8:(k + 1) * 8],
                                    xnf[:, kc * T + tt * TTA + s * 128:
                                        kc * T + tt * TTA + (s + 1) * 128],
                                    gwt_sb[:, kc * E:(kc + 1) * E],
                                    start=(kc == 0), stop=(kc == 7))
                        return last
                    sched.append(("t", f_rpe))

                    def f_rm(v):
                        v.reduce_max(rt[:, 0:1], pr[:, 0:8], axis=AX,
                                     negate=True)
                        return v.reduce_max(rt[:, 32:33], pr[:, 8:16], axis=AX,
                                            negate=True)
                    sched.append(("v", f_rm))

                    def f_re(a, tt=tt, sp=sp):
                        act(a, rt[:, 8:16], pr[:, 0:8], AF.Exp,
                                     bias=rt[:, 0:1])
                        return act(a, rt[:, 40:48], pr[:, 8:16], AF.Exp,
                                            bias=rt[:, 32:33])
                    sched.append(("a", f_re))

                    def f_rc(v, tt=tt, sp=sp):
                        def ops(B, g128):
                            yield lambda: v.reduce_sum(rt[:, B + 1:B + 2],
                                                       rt[:, B + 8:B + 16],
                                                       axis=AX)
                            yield lambda: v.reciprocal(rt[:, B + 2:B + 3],
                                                       rt[:, B + 1:B + 2])
                            yield lambda: v.tensor_scalar(
                                rt[:, B + 8:B + 16], rt[:, B + 8:B + 16],
                                rt[:, B + 2:B + 3], None, ALU.mult)
                            yield lambda: v.reduce_max(rt[:, B + 3:B + 4],
                                                       rt[:, B + 8:B + 16],
                                                       axis=AX)
                            yield lambda: v.tensor_scalar(
                                rt[:, B + 16:B + 24], rt[:, B + 8:B + 16],
                                rt[:, B + 3:B + 4], None, ALU.is_ge)
                            yield lambda: v.tensor_scalar(
                                rt[:, B + 24:B + 32], rt[:, B + 16:B + 24],
                                -1.0, 1.0, ALU.mult, ALU.add)
                            yield lambda: v.tensor_mul(
                                rt[:, B + 24:B + 32], rt[:, B + 8:B + 16],
                                rt[:, B + 24:B + 32])
                            yield lambda: v.reduce_max(rt[:, B + 4:B + 5],
                                                       rt[:, B + 24:B + 32],
                                                       axis=AX)
                            yield lambda: v.tensor_scalar(
                                rt[:, B + 24:B + 32], rt[:, B + 24:B + 32],
                                rt[:, B + 4:B + 5], None, ALU.is_ge)
                            yield lambda: v.tensor_add(rt[:, B + 5:B + 6],
                                                       rt[:, B + 3:B + 4],
                                                       rt[:, B + 4:B + 5])
                            yield lambda: v.reciprocal(rt[:, B + 6:B + 7],
                                                       rt[:, B + 5:B + 6])
                            yield lambda: v.tensor_mul(rt[:, B + 3:B + 4],
                                                       rt[:, B + 3:B + 4],
                                                       rt[:, B + 6:B + 7])
                            yield lambda: v.tensor_mul(rt[:, B + 4:B + 5],
                                                       rt[:, B + 4:B + 5],
                                                       rt[:, B + 6:B + 7])
                            yield lambda: v.tensor_scalar(
                                rt[:, B + 16:B + 24], rt[:, B + 16:B + 24],
                                rt[:, B + 3:B + 4], None, ALU.mult)
                            yield lambda: v.tensor_scalar(
                                rt[:, B + 24:B + 32], rt[:, B + 24:B + 32],
                                rt[:, B + 4:B + 5], None, ALU.mult)
                            yield lambda: v.tensor_add(
                                combine[:, g128 * 8:(g128 + 1) * 8],
                                rt[:, B + 16:B + 24], rt[:, B + 24:B + 32])
                        g0 = tt * 4 + sp * 2
                        it0 = ops(0, g0)
                        it1 = ops(32, g0 + 1)
                        last = None
                        for o0, o1 in zip(it0, it1):
                            last = o0()
                            last = o1()
                        return last
                    sched.append(("v", f_rc))

            for tt in range(NTT):
                emit_qg(tt)
            _sched_run(nc, sQG, sched)

        with ExitStack() as a2s:
            def A2SB(name, shape, dt):
                return a2s.enter_context(nc.sbuf_tensor(name, shape, dt))
            l_h0r = A2SB("l_h0r", [128, TTA], F32)
            l_h0i = A2SB("l_h0i", [128, TTA], F32)
            l_acr = A2SB("l_acr", [128, TTA], F32)
            l_aci = A2SB("l_aci", [128, TTA], F32)
            l_q = A2SB("l_q", [128, TTA], F32)
            l_sg = A2SB("l_sg", [128, TTA], F32)
            w_1 = A2SB("w_1", [128, TTA], F32)
            w_2 = A2SB("w_2", [128, TTA], F32)
            w_3 = A2SB("w_3", [128, TTA], F32)
            zt = A2SB("zt", [128, TTA], BF16)
            acc4 = A2SB("acc4", [128, 4], F32)

            sched = []

            def emit_a2(tt, dc):
                t0 = tt * TTA

                def f_l(s, dc=dc, t0=t0):
                    for qi, dst in ((0, l_h0r), (1, l_h0i), (2, l_acr),
                                    (3, l_aci), (4, l_q), (5, l_sg)):
                        s.dma_start(dst[:, :],
                                    stash_d[qi, dc, :, t0:t0 + TTA]
                                    ).then_inc(sQL, 16)
                        cnt["QL"] += 16
                    s.wait_ge(sQL, cnt["QL"])
                    return None
                sched.append(("s", f_l))

                def f_d(v, dc=dc):
                    hgr = hg[:, 2 * dc:2 * dc + 1]
                    hgi = hg[:, 2 * dc + 1:2 * dc + 2]
                    v.tensor_scalar(w_1[:, :], l_acr[:, :], hgr, None, ALU.mult)
                    v.tensor_scalar(w_2[:, :], l_aci[:, :], hgi, None, ALU.mult)
                    v.tensor_sub(w_1[:, :], w_1[:, :], w_2[:, :])
                    v.tensor_add(w_1[:, :], w_1[:, :], l_h0r[:, :])
                    v.tensor_scalar(w_2[:, :], l_acr[:, :], hgi, None, ALU.mult)
                    v.tensor_scalar(w_3[:, :], l_aci[:, :], hgr, None, ALU.mult)
                    v.tensor_add(w_2[:, :], w_2[:, :], w_3[:, :])
                    v.tensor_add(w_2[:, :], w_2[:, :], l_h0i[:, :])
                    v.tensor_mul(w_1[:, :], l_q[:, :], w_1[:, :])
                    v.tensor_mul(w_2[:, :], l_q[:, :], w_2[:, :])
                    return v.tensor_mul(zt[:, :], w_1[:, :], l_sg[:, :])
                sched.append(("v", f_d))

                def f_a(a):
                    act(a, w_3[:, :], w_1[:, :], AF.Copy,
                                 accum_out=acc4[:, 0:1])
                    act(a, w_3[:, :], w_2[:, :], AF.Copy,
                                 accum_out=acc4[:, 1:2])
                    act(a, w_3[:, :], w_1[:, :], AF.Square,
                                 accum_out=acc4[:, 2:3])
                    return act(a, w_3[:, :], w_2[:, :], AF.Square,
                                        accum_out=acc4[:, 3:4])
                sched.append(("a", f_a))

                def f_d2(v, dc=dc):
                    return v.tensor_add(stats_acc[:, dc * 4:(dc + 1) * 4],
                                        stats_acc[:, dc * 4:(dc + 1) * 4],
                                        acc4[:, :])
                sched.append(("v", f_d2))

                def f_s(g, dc=dc, t0=t0):
                    g.dma_start(z_d[dc, :, t0:t0 + TTA], zt[:, :]
                                ).then_inc(sQS, 16)
                    cnt["QS"] += 16
                    g.wait_ge(sQS, cnt["QS"])
                    return None
                sched.append(("g", f_s))

            for tt in range(NTT):
                for dc in range(DCN):
                    emit_a2(tt, dc)
            _sched_run(nc, sA2, sched)

        with ExitStack() as x2s:
            sg43 = x2s.enter_context(nc.sbuf_tensor("sg43", [4, 4], F32))
            sg43b = x2s.enter_context(nc.sbuf_tensor("sg43b", [4, 4], F32))
            ps43 = x2s.enter_context(nc.psum_tensor("ps43", [4, 4], F32))
            row0 = x2s.enter_context(nc.sbuf_tensor("row0", [1, 128], F32))
            row1 = x2s.enter_context(nc.sbuf_tensor("row1", [1, 128], F32))
            grw = x2s.enter_context(nc.sbuf_tensor("grw", [1, 256], F32))
            gnsr = x2s.enter_context(nc.sbuf_tensor("gnsr_sb", [1, D], F32))
            gnbr = x2s.enter_context(nc.sbuf_tensor("gnbr_sb", [1, D], F32))
            c1row = x2s.enter_context(nc.sbuf_tensor("c1row", [1, D], F32))
            c2row = x2s.enter_context(nc.sbuf_tensor("c2row", [1, D], F32))

            for dc in range(DCN):
                with nc.Block() as blk:
                    @blk.tensor
                    def _(t, dc=dc):
                        t.matmul(ps43[:, :], gmap[:, :],
                                 stats_acc[:, dc * 4:(dc + 1) * 4],
                                 start=True, stop=True)
                with nc.Block() as blk:
                    @blk.vector
                    def _(v):
                        v.tensor_scalar(sg43[:, :], ps43[:, :],
                                        bsel[0:4, 0:1], None, ALU.mult)
                        v.tensor_scalar(sg43b[:, :], ps43[:, :],
                                        bsel[0:4, 1:2], None, ALU.mult)
                with nc.Block() as blk:
                    @blk.gpsimd
                    def _(g, dc=dc):
                        g.dma_start(
                            stats_in_d.ap()[0:1, dc * 16:(dc + 1) * 16]
                            .rearrange("o (a b) -> a (o b)", a=4),
                            sg43[:, :]).then_inc(sQS, 16)
                        g.dma_start(
                            stats_in_d.ap()[1:2, dc * 16:(dc + 1) * 16]
                            .rearrange("o (a b) -> a (o b)", a=4),
                            sg43b[:, :]).then_inc(sQS, 16)
                        cnt["QS"] += 32
                        g.wait_ge(sQS, cnt["QS"])

            with nc.Block() as blk:
                @blk.gpsimd
                def _(g):
                    g.collective_compute(
                        "AllReduce", ALU.add,
                        replica_groups=[list(range(8))],
                        ins=[stats_in_d.ap().opt()],
                        outs=[stats_out_d.ap().opt()]).then_inc(sCC, 1)
                    cnt["CC"] += 1
                    g.wait_ge(sCC, cnt["CC"])

            with nc.Block() as blk:
                @blk.sync
                def _(s):
                    s.dma_start(row0[:, :], stats_out_d[0:1, :]).then_inc(sQL, 16)
                    s.dma_start(row1[:, :], stats_out_d[1:2, :]).then_inc(sQL, 16)
                    s.dma_start(gnsr[:, :], gnsr_d[:, :]).then_inc(sQL, 16)
                    s.dma_start(gnbr[:, :], gnbr_d[:, :]).then_inc(sQL, 16)
                    cnt["QL"] += 64
                    s.wait_ge(sQL, cnt["QL"])

            with nc.Block() as blk:
                @blk.vector
                def _(v):
                    v.tensor_scalar(row0[:, :], row0[:, :], bsel[0:1, 0:1],
                                    None, ALU.mult)
                    v.tensor_scalar(row1[:, :], row1[:, :], bsel[0:1, 1:2],
                                    None, ALU.mult)
                    v.tensor_add(row0[:, :], row0[:, :], row1[:, :])
                    r4 = row0.ap().rearrange("o (g j) -> o g j", g=32)
                    v.tensor_tensor(grw.ap()[0:1, 160:192].unsqueeze(2),
                                    r4[:, :, 2:3],
                                    r4[:, :, 3:4], ALU.add)
                    v.tensor_scalar(grw.ap()[0:1, 0:32].unsqueeze(2),
                                    r4[:, :, 0:1], CNT_INV,
                                    None, ALU.mult)
                    v.tensor_scalar(grw.ap()[0:1, 32:64].unsqueeze(2),
                                    r4[:, :, 1:2], CNT_INV,
                                    None, ALU.mult)
                    v.tensor_scalar(grw[0:1, 64:96], grw[0:1, 160:192],
                                    CNT_INV, None, ALU.mult)
                    v.tensor_mul(grw[0:1, 96:128], grw[0:1, 0:32],
                                 grw[0:1, 0:32])
                    v.tensor_mul(grw[0:1, 128:160], grw[0:1, 32:64],
                                 grw[0:1, 32:64])
                    v.tensor_sub(grw[0:1, 64:96], grw[0:1, 64:96],
                                 grw[0:1, 96:128])
                    v.memset(grw[0:1, 224:225], 0.0)
                    v.tensor_sub(grw[0:1, 64:96], grw[0:1, 64:96],
                                 grw[0:1, 128:160])

            with nc.Block() as blk:
                @blk.scalar
                def _(a):
                    act(a, grw[0:1, 96:128], grw[0:1, 64:96], AF.Sqrt,
                                 bias=epsc[0:1, 1:2])
            with nc.Block() as blk:
                @blk.vector
                def _(v):
                    v.reciprocal(grw[0:1, 192:224], grw[0:1, 96:128])
                    rstd_b = grw.ap()[0:1, 192:224].unsqueeze(2).broadcast_to(
                        (1, 32, 32))
                    mre_b = grw.ap()[0:1, 0:32].unsqueeze(2).broadcast_to(
                        (1, 32, 32))
                    g3 = gnsr.ap().rearrange("o (g j) -> o g j", g=32)
                    c13 = c1row.ap().rearrange("o (g j) -> o g j", g=32)
                    c23 = c2row.ap().rearrange("o (g j) -> o g j", g=32)
                    v.memset(grw[0:1, 224:225], 0.0)
                    v.tensor_tensor(c13, g3, rstd_b, ALU.mult)
                    v.memset(grw[0:1, 225:226], 0.0)
                    v.tensor_tensor(c23, c13, mre_b, ALU.mult)
                    v.memset(grw[0:1, 226:227], 0.0)
                    v.tensor_sub(c2row[:, :], gnbr[:, :], c2row[:, :])

            with nc.Block() as blk:
                @blk.gpsimd
                def _(g):
                    g.dma_start(crow_d[0:1, :], c1row[:, :]).then_inc(sQS, 16)
                    g.dma_start(crow_d[1:2, :], c2row[:, :]).then_inc(sQS, 16)
                    cnt["QS"] += 32
                    g.wait_ge(sQS, cnt["QS"])
            with nc.Block() as blk:
                @blk.sync
                def _(s):
                    for dc in range(DCN):
                        s.dma_start(c1pp[:, dc:dc + 1],
                                    crow_d[0:1, dc * 128:(dc + 1) * 128]
                                    .rearrange("o p -> p o")).then_inc(sQL, 16)
                        s.dma_start(c2pp[:, dc:dc + 1],
                                    crow_d[1:2, dc * 128:(dc + 1) * 128]
                                    .rearrange("o p -> p o")).then_inc(sQL, 16)
                        cnt["QL"] += 32
                    s.wait_ge(sQL, cnt["QL"])

        with ExitStack() as wos:
            def WOSB(name, shape, dt):
                return wos.enter_context(nc.sbuf_tensor(name, shape, dt))
            wo_sb = WOSB("wo_sb", [128, 8 * D], BF16)
            zl = WOSB("zl", [128, DCN * TTA], BF16)
            sgl = WOSB("sgl", [128, DCN * TTA], F32)
            mfm = WOSB("mfm", [128, DCN * TTA], BF16)
            mt1 = WOSB("mt1", [128, TTA], F32)
            mt2 = WOSB("mt2", [128, TTA], F32)
            woo = WOSB("woo", [128, 4 * D], F32)
            wo_ps = wos.enter_context(nc.psum_tensor("wo_ps", [128, 4 * D], F32))

            with nc.Block() as blk:
                @blk.gpsimd
                def _(g):
                    g.dma_start(wo_sb.ap().rearrange("p (c m) -> p c m", c=8),
                                wo_d.ap().rearrange("(c p) m -> p c m", p=128)
                                ).then_inc(sQC, 16)
                    cnt["QC"] += 16
                    g.wait_ge(sQC, cnt["QC"])

            sched = []

            def emit_wo(tt):
                t0 = tt * TTA

                def f_l(s, t0=t0):
                    for dc in range(DCN):
                        s.dma_start(zl[:, dc * TTA:(dc + 1) * TTA],
                                    z_d[dc, :, t0:t0 + TTA]).then_inc(sQL, 16)
                        s.dma_start(sgl[:, dc * TTA:(dc + 1) * TTA],
                                    stash_d[5, dc, :, t0:t0 + TTA]
                                    ).then_inc(sQL, 16)
                        cnt["QL"] += 32
                    s.wait_ge(sQL, cnt["QL"])
                    return None
                sched.append(("s", f_l))

                def f_m(v):
                    last = None
                    for dc in range(DCN):
                        v.tensor_scalar(mt1[:, :], zl[:, dc * TTA:(dc + 1) * TTA],
                                        c1pp[:, dc:dc + 1], None, ALU.mult)
                        v.tensor_scalar(mt2[:, :],
                                        sgl[:, dc * TTA:(dc + 1) * TTA],
                                        c2pp[:, dc:dc + 1], None, ALU.mult)
                        last = v.tensor_add(mfm[:, dc * TTA:(dc + 1) * TTA],
                                            mt1[:, :], mt2[:, :])
                    return last
                sched.append(("v", f_m))

                def f_pe(t):
                    last = None
                    for s in range(4):
                        for half in range(2):
                            for dc in range(DCN):
                                last = t.matmul(
                                    wo_ps[:, s * D + half * 512:
                                          s * D + (half + 1) * 512],
                                    mfm[:, dc * TTA + s * 128:
                                        dc * TTA + (s + 1) * 128],
                                    wo_sb[:, dc * D + half * 512:
                                          dc * D + (half + 1) * 512],
                                    start=(dc == 0), stop=(dc == 7))
                    return last
                sched.append(("t", f_pe))

                def f_ev(v):
                    return v.tensor_copy(woo[:, :], wo_ps[:, :])
                sched.append(("v", f_ev))

                def f_s(g, tt=tt):
                    for s in range(4):
                        g.dma_start(wo_out_d[tt * 4 + s, :, :],
                                    woo[:, s * D:(s + 1) * D]).then_inc(sQS, 16)
                        cnt["QS"] += 16
                    g.wait_ge(sQS, cnt["QS"])
                    return None
                sched.append(("g", f_s))

            for tt in range(NTT):
                emit_wo(tt)
            _sched_run(nc, sWO, sched)

        with ExitStack() as gus:
            def GUSB(name, shape, dt):
                return gus.enter_context(nc.sbuf_tensor(name, shape, dt))
            slg = [GUSB(f"slg{i}", [128, 8 * 512], BF16) for i in range(2)]
            slu = [GUSB(f"slu{i}", [128, 8 * 512], BF16) for i in range(2)]
            geltmp = [GUSB(f"gel{i}", [128, TTA], BF16) for i in range(2)]
            hactb = [GUSB(f"hactb{i}", [128, TTA], BF16) for i in range(4)]
            psg = [gus.enter_context(nc.psum_tensor(f"psg{i}", [128, TTA], F32))
                   for i in range(2)]
            psu = [gus.enter_context(nc.psum_tensor(f"psu{i}", [128, TTA], F32))
                   for i in range(2)]

            NIT = E * 8
            GPI = 8 * 4

            def gu_slab_load(g, it):
                e, fb = divmod(it, 8)
                p = it % 2
                g.dma_start(
                    slg[p].ap().rearrange("p (c f) -> p c f", c=8),
                    guT_d[e, :, fb * 512:(fb + 1) * 512]
                    .rearrange("(c p) f -> p c f", p=128)).then_inc(sGW[p], 16)
                g.dma_start(
                    slu[p].ap().rearrange("p (c f) -> p c f", c=8),
                    guT_d[e, :, FF + fb * 512:FF + (fb + 1) * 512]
                    .rearrange("(c p) f -> p c f", p=128)).then_inc(sGW[p], 16)

            with nc.Block() as blk:
                @blk.gpsimd
                def _(g):
                    gu_slab_load(g, 0)
                    gu_slab_load(g, 1)
                    for it in range(NIT):
                        e, fb = divmod(it, 8)
                        for grp in range(GPI):
                            gg = it * GPI + grp
                            g.wait_ge(sDV1, gg + 1)
                            tc, fc = divmod(grp, 4)
                            t0 = tc * TTA
                            g.dma_start(
                                hact_d[e, fb * 4 + fc, :, t0:t0 + TTA],
                                hactb[gg % 4][:, :]).then_inc(sHAb[gg % 4], 16)
                        if it + 2 < NIT:
                            gu_slab_load(g, it + 2)

                @blk.tensor
                def _(t):
                    for it in range(NIT):
                        p = it % 2
                        t.wait_ge(sGW[p], 32 * (it // 2 + 1))
                        for grp in range(GPI):
                            gg = it * GPI + grp
                            tc, fc = divmod(grp, 4)
                            pp = gg % 2
                            if gg >= 2:
                                t.wait_ge(sAC1, gg - 1)
                                t.wait_ge(sDV1, gg - 1)
                            for hc in range(8):
                                t.matmul(psg[pp][:, :],
                                         slg[p][:, hc * 512 + fc * 128:
                                                hc * 512 + (fc + 1) * 128],
                                         xnf[:, hc * T + tc * TTA:
                                             hc * T + (tc + 1) * TTA],
                                         start=(hc == 0), stop=(hc == 7))
                            last = None
                            for hc in range(8):
                                last = t.matmul(
                                    psu[pp][:, :],
                                    slu[p][:, hc * 512 + fc * 128:
                                           hc * 512 + (fc + 1) * 128],
                                    xnf[:, hc * T + tc * TTA:
                                        hc * T + (tc + 1) * TTA],
                                    start=(hc == 0), stop=(hc == 7))
                            last.then_inc(sPE1, 1)

                @blk.scalar
                def _(a):
                    for gg in range(NIT * GPI):
                        pp = gg % 2
                        a.wait_ge(sPE1, gg + 1)
                        act(a, geltmp[pp][:, :], psg[pp][:, :],
                                     AF.Gelu).then_inc(sAC1, 1)

                @blk.vector
                def _(v):
                    for gg in range(NIT * GPI):
                        pp = gg % 2
                        hb = gg % 4
                        v.wait_ge(sAC1, gg + 1)
                        if gg >= 4:
                            v.wait_ge(sHAb[hb], 16 * ((gg - 4) // 4 + 1))
                        v.tensor_mul(hactb[hb][:, :], psu[pp][:, :],
                                     geltmp[pp][:, :]).then_inc(sDV1, 1)

        if dbg:
            with nc.Block() as blk:
                @blk.gpsimd
                def _(g):
                    g.dma_start(dbg_xnf[:, :], xnf[:, :]).then_inc(sQS, 16)
                    g.dma_start(dbg_st4[:, :], st4c_ref[:, :]).then_inc(sQS, 16)
                    cnt["QS"] += 32
                    g.wait_ge(sQS, cnt["QS"])

        xnscope.close()

        with nc.Block() as blk:
            @blk.gpsimd
            def _(g):
                g.wait_ge(sHAb[0], 16 * 512)
                g.wait_ge(sHAb[1], 16 * 512)
                g.wait_ge(sHAb[2], 16 * 512)
                g.wait_ge(sHAb[3], 16 * 512)

        with ExitStack() as dns:
            def DNSB(name, shape, dt):
                return dns.enter_context(nc.sbuf_tensor(name, shape, dt))
            dts = [DNSB(f"dts{i}", [128, 32 * 1024], BF16) for i in range(2)]
            hls = [DNSB(f"hls{i}", [128, 32 * 128], BF16) for i in range(2)]
            acc = DNSB("dacc", [128, 8 * 1024], F32)
            gmsb = DNSB("gmsb", [128, 8 * 1024], BF16)
            psd = [dns.enter_context(nc.psum_tensor(f"psd{i}", [128, 1024], F32))
                   for i in range(2)]

            NITD = 4 * E

            def dt_slab_load(g, it):
                e = it % E
                p = it % 2
                g.dma_start(
                    dts[p].ap().rearrange("p (c h) -> p c h", c=32),
                    dwT_d[e].rearrange("(c p) h -> p c h", p=128)
                    ).then_inc(sDW[p], 16)

            with nc.Block() as blk:
                @blk.gpsimd
                def _(g):
                    dt_slab_load(g, 0)
                    dt_slab_load(g, 1)
                    for it in range(NITD - 2):
                        g.wait_ge(sPE2, (it + 1) * 8)
                        dt_slab_load(g, it + 2)

                @blk.sync
                def _(s):
                    for it in range(NITD):
                        tcb = it // E
                        e = it % E
                        for t8 in range(8):
                            j = it * 8 + t8
                            if j >= 2:
                                s.wait_ge(sPE2, j - 1)
                            t128 = tcb * 8 + t8
                            s.dma_start(
                                hls[j % 2].ap().rearrange("p (c t) -> p c t",
                                                          c=32),
                                hact_d[e, :, :, t128 * 128:(t128 + 1) * 128]
                                .rearrange("c p t -> p c t")).then_inc(sDH, 16)

                @blk.tensor
                def _(t):
                    for it in range(NITD):
                        p = it % 2
                        t.wait_ge(sDW[p], 16 * (it // 2 + 1))
                        for t8 in range(8):
                            j = it * 8 + t8
                            t.wait_ge(sDH, (j + 1) * 16)
                            if j >= 2:
                                t.wait_ge(sDV2, j - 1)
                            last = None
                            for fc in range(32):
                                t.matmul(
                                    psd[j % 2][:, 0:512],
                                    hls[j % 2][:, fc * 128:(fc + 1) * 128],
                                    dts[p][:, fc * 1024:fc * 1024 + 512],
                                    start=(fc == 0), stop=(fc == 31))
                                last = t.matmul(
                                    psd[j % 2][:, 512:1024],
                                    hls[j % 2][:, fc * 128:(fc + 1) * 128],
                                    dts[p][:, fc * 1024 + 512:(fc + 1) * 1024],
                                    start=(fc == 0), stop=(fc == 31))
                            last.then_inc(sPE2, 1)

                @blk.vector
                def _(v):
                    for it in range(NITD):
                        e = it % E
                        tcb = it // E
                        for t8 in range(8):
                            j = it * 8 + t8
                            t128 = tcb * 8 + t8
                            v.wait_ge(sPE2, j + 1)
                            if e == 0 and it >= 8 and t8 == 0:
                                v.wait_ge(sSI, it // 8)
                            cslice = combine[:, t128 * 8 + e:t128 * 8 + e + 1]
                            if e == 0:
                                v.tensor_scalar(
                                    acc[:, t8 * 1024:(t8 + 1) * 1024],
                                    psd[j % 2][:, :], cslice, None,
                                    ALU.mult).then_inc(sDV2, 1)
                            else:
                                v.scalar_tensor_tensor(
                                    acc[:, t8 * 1024:(t8 + 1) * 1024],
                                    psd[j % 2][:, :], cslice,
                                    acc[:, t8 * 1024:(t8 + 1) * 1024],
                                    ALU.mult, ALU.add).then_inc(sDV2, 1)

                @blk.scalar
                def _(a):
                    nsil = 0
                    nsj = 0
                    for it in range(NITD):
                        tcb = it // E
                        e = it % E
                        if e != E - 1:
                            continue
                        a.wait_ge(sDV2, (it + 1) * 8)
                        lastsil = None
                        for t8 in range(8):
                            lastsil = act(a, gmsb[:, t8 * 1024:(t8 + 1) * 1024],
                                          acc[:, t8 * 1024:(t8 + 1) * 1024],
                                          AF.Silu)
                        lastsil.then_inc(sSJ, 1)
                        nsj += 1
                        a.wait_ge(sSJ, nsj)
                        for t8 in range(8):
                            t128 = tcb * 8 + t8
                            a.dma_start(
                                gmoe_d[t128, :, :],
                                gmsb[:, t8 * 1024:(t8 + 1) * 1024]
                            ).then_inc(sQS, 16)
                            cnt["QS"] += 16
                        a.wait_ge(sQS, cnt["QS"])
                        if dbg and it == 7:
                            a.dma_start(dbg_acc0[:, :], acc[:, 0:4096]
                                        ).then_inc(sQS, 16)
                            cnt["QS"] += 16
                            a.wait_ge(sQS, cnt["QS"])
                        nsil += 1
                        a.sem_inc(sSI, 1)
                    if dbg:
                        a.dma_start(dbg_acc[:, :], acc[:, 0:4096]
                                    ).then_inc(sQS, 16)
                        cnt["QS"] += 16
                        a.wait_ge(sQS, cnt["QS"])

        with ExitStack() as bs:
            def BSB(name, shape, dt):
                return bs.enter_context(nc.sbuf_tensor(name, shape, dt))
            w1_sb = BSB("w1_sb", [128, 8 * FF], BF16)
            w2_sb = BSB("w2_sb", [128, 32 * D], BF16)
            b1c = BSB("b1c_sb", [128, 32], F32)
            b2b = BSB("b2b_sb", [128, D], BF16)
            l1sb = BSB("l1sb_sb", [128, D], BF16)
            l1bb = BSB("l1bb_sb", [128, D], BF16)
            l2sb = BSB("l2sb_sb", [128, D], BF16)
            l2bb = BSB("l2bb_sb", [128, D], BF16)
            xb_tm = BSB("xb_tm", [128, 2 * D], F32)
            wol = BSB("wol", [128, 2 * D], F32)
            gml = BSB("gml", [128, 2 * D], BF16)
            t1b = BSB("t1b", [128, 2 * D], F32)
            xh_tm = BSB("xh_tm", [128, 2 * D], F32)
            xh_bf = BSB("xh_bf", [128, 2 * D], BF16)
            xh_fm = BSB("xh_fm", [128, 8 * TTB], BF16)
            hh_fm = BSB("hh_fm", [128, 32 * TTB], BF16)
            stb = BSB("stb", [128, 32], F32)
            ps_big = bs.enter_context(nc.psum_tensor("ps_big", [128, 2 * D], F32))
            trb_ps = bs.enter_context(nc.psum_tensor("trb_ps", [128, 2 * D], BF16))
            p1 = [bs.enter_context(nc.psum_tensor(f"p1_{i}", [128, TTB], F32))
                  for i in range(2)]

            with nc.Block() as blk:
                @blk.gpsimd
                def _(g):
                    g.dma_start(w1_sb.ap().rearrange("p (c m) -> p c m", c=8),
                                w1_d.ap().rearrange("(c p) m -> p c m", p=128)
                                ).then_inc(sQC, 16)
                    g.dma_start(w2_sb.ap().rearrange("p (c m) -> p c m", c=32),
                                w2_d.ap().rearrange("(c p) m -> p c m", p=128)
                                ).then_inc(sQC, 16)
                    g.dma_start(b1c[:, :], b1c_d[:, :]).then_inc(sQC, 16)
                    g.dma_start(b2b[:, :], b2b_d[:, :]).then_inc(sQC, 16)
                    g.dma_start(l1sb[:, :], l1sb_d[:, :]).then_inc(sQC, 16)
                    g.dma_start(l1bb[:, :], l1bb_d[:, :]).then_inc(sQC, 16)
                    g.dma_start(l2sb[:, :], l2sb_d[:, :]).then_inc(sQC, 16)
                    g.dma_start(l2bb[:, :], l2bb_d[:, :]).then_inc(sQC, 16)
                    cnt["QC"] += 128
                    g.wait_ge(sQC, cnt["QC"])

            sched = []

            def emit_ln_tm(sched, src, dst, eps_ap, scale_t, bias_t):
                nsub = 2

                def f1(v, src=src):
                    return v.reduce_sum(
                        stb[:, 0:2],
                        src.ap().rearrange("p (s d) -> p s d", s=nsub), axis=AX)
                sched.append(("v", f1))

                def f2(a, src=src, dst=dst):
                    last = None
                    for s in range(nsub):
                        last = act(a, dst[:, s * D:(s + 1) * D],
                                            src[:, s * D:(s + 1) * D],
                                            AF.Square,
                                            accum_out=stb[:, 4 + s:5 + s])
                    return last
                sched.append(("a", f2))

                def f3(v):
                    v.tensor_scalar(stb[:, 2:4], stb[:, 0:2], 1.0 / D, None,
                                    ALU.mult)
                    v.tensor_scalar(stb[:, 6:8], stb[:, 4:6], 1.0 / D, None,
                                    ALU.mult)
                    v.tensor_mul(stb[:, 8:10], stb[:, 2:4], stb[:, 2:4])
                    v.memset(stb[:, 30:31], 0.0)
                    return v.tensor_sub(stb[:, 8:10], stb[:, 6:8], stb[:, 8:10])
                sched.append(("v", f3))

                def f4(a, eps_ap=eps_ap):
                    return act(a, stb[:, 10:12], stb[:, 8:10], AF.Sqrt,
                                        bias=eps_ap)
                sched.append(("a", f4))

                def f5(v, src=src, dst=dst, scale_t=scale_t, bias_t=bias_t):
                    v.reciprocal(stb[:, 12:14], stb[:, 10:12])
                    v.memset(stb[:, 30:31], 0.0)
                    last = None
                    for s in range(nsub):
                        last = v.tensor_scalar(dst[:, s * D:(s + 1) * D],
                                               src[:, s * D:(s + 1) * D],
                                               stb[:, 2 + s:3 + s],
                                               stb[:, 12 + s:13 + s],
                                               ALU.subtract, ALU.mult)
                    if scale_t is not None:
                        for s in range(nsub):
                            v.tensor_mul(dst[:, s * D:(s + 1) * D],
                                         dst[:, s * D:(s + 1) * D],
                                         scale_t[:, :])
                            last = v.tensor_add(dst[:, s * D:(s + 1) * D],
                                                dst[:, s * D:(s + 1) * D],
                                                bias_t[:, :])
                    return last
                sched.append(("v", f5))

            def emit_b(ttb):
                t0 = ttb * TTB

                def f_l(s, ttb=ttb, t0=t0):
                    s.dma_start(
                        xb_tm.ap().rearrange("p (s d) -> p s d", s=2),
                        x_d[t0:t0 + TTB, :].rearrange("(s p) d -> p s d", p=128)
                    ).then_inc(sQL, 16)
                    cnt["QL"] += 16
                    for s2 in range(2):
                        s.dma_start(wol[:, s2 * D:(s2 + 1) * D],
                                    wo_out_d[ttb * 2 + s2, :, :]
                                    ).then_inc(sQL, 16)
                        s.dma_start(gml[:, s2 * D:(s2 + 1) * D],
                                    gmoe_d[ttb * 2 + s2, :, :]
                                    ).then_inc(sQL, 16)
                        cnt["QL"] += 32
                    s.wait_ge(sQL, cnt["QL"])
                    return None
                sched.append(("s", f_l))

                emit_ln_tm(sched, xb_tm, t1b, epsc[:, 0:1], None, None)

                def f_t1(v):
                    return v.tensor_add(t1b[:, :], t1b[:, :], wol[:, :])
                sched.append(("v", f_t1))

                emit_ln_tm(sched, t1b, xh_tm, epsc[:, 1:2], l1sb, l1bb)

                def f_cp(v):
                    return v.tensor_copy(xh_bf[:, :], xh_tm[:, :])
                sched.append(("v", f_cp))

                def f_tr(t):
                    last = None
                    for s in range(2):
                        for dc in range(DCN):
                            last = t.transpose(
                                trb_ps[:, dc * TTB + s * 128:
                                       dc * TTB + (s + 1) * 128],
                                xh_bf[:, s * D + dc * 128:s * D + (dc + 1) * 128],
                                ident[:, :])
                    return last
                sched.append(("t", f_tr))

                def f_tre(v):
                    return v.tensor_copy(xh_fm[:, :], trb_ps[:, :])
                sched.append(("v", f_tre))

                for fcp in range(16):
                    def f_w1(t, fcp=fcp):
                        last = None
                        for i in range(2):
                            fc = fcp * 2 + i
                            for kc in range(8):
                                last = t.matmul(
                                    p1[i][:, :],
                                    w1_sb[:, kc * FF + fc * 128:
                                          kc * FF + (fc + 1) * 128],
                                    xh_fm[:, kc * TTB:(kc + 1) * TTB],
                                    start=(kc == 0), stop=(kc == 7))
                        return last
                    sched.append(("t", f_w1))

                    def f_g1(a, fcp=fcp):
                        last = None
                        for i in range(2):
                            fc = fcp * 2 + i
                            last = act(a, 
                                hh_fm[:, fc * TTB:(fc + 1) * TTB],
                                p1[i][:, :], AF.Gelu_apprx_tanh,
                                bias=b1c[:, fc:fc + 1])
                        return last
                    sched.append(("a", f_g1))

                for s in range(2):
                    def f_w2(t, s=s):
                        last = None
                        for half in range(2):
                            for fc in range(32):
                                last = t.matmul(
                                    ps_big[:, s * D + half * 512:
                                           s * D + (half + 1) * 512],
                                    hh_fm[:, fc * TTB + s * 128:
                                          fc * TTB + (s + 1) * 128],
                                    w2_sb[:, fc * D + half * 512:
                                          fc * D + (half + 1) * 512],
                                    start=(fc == 0), stop=(fc == 31))
                        return last
                    sched.append(("t", f_w2))

                def f_res(v):
                    for s in range(2):
                        v.scalar_tensor_tensor(
                            wol[:, s * D:(s + 1) * D],
                            ps_big[:, s * D:(s + 1) * D], 1.0, b2b[:, :],
                            ALU.mult, ALU.add)
                    return v.tensor_add(wol[:, :], wol[:, :], xh_tm[:, :])
                sched.append(("v", f_res))

                emit_ln_tm(sched, wol, t1b, epsc[:, 1:2], l2sb, l2bb)

                def f_fin(v):
                    v.tensor_mul(t1b[:, :], t1b[:, :], gml[:, :])
                    return v.tensor_add(wol[:, :], xb_tm[:, :], t1b[:, :])
                sched.append(("v", f_fin))

                def f_out(g, t0=t0):
                    g.dma_start(
                        y_d[t0:t0 + TTB, :].rearrange("(s p) d -> p s d", p=128),
                        wol.ap().rearrange("p (s d) -> p s d", s=2)
                    ).then_inc(sQO, 16)
                    cnt["QO"] += 16
                    return None
                sched.append(("g", f_out))

            for ttb in range(NTTB):
                emit_b(ttb)
            _sched_run(nc, sB, sched)

        if dbg:
            with nc.Block() as blk:
                @blk.gpsimd
                def _(g):
                    g.dma_start(dbg_comb[:, :], combine[:, :]).then_inc(sQS, 16)
                    cnt["QS"] += 16
                    g.dma_start(dbg_stash.ap(), stash_d.ap()).then_inc(sQS, 16)
                    g.dma_start(dbg_z.ap(), z_d.ap()).then_inc(sQS, 16)
                    g.dma_start(dbg_summ.ap(), summ_ag_d.ap()).then_inc(sQS, 16)
                    g.dma_start(dbg_crow.ap(), crow_d.ap()).then_inc(sQS, 16)
                    g.dma_start(dbg_wo.ap(), wo_out_d.ap()).then_inc(sQS, 16)
                    g.dma_start(dbg_gm.ap(), gmoe_d.ap()).then_inc(sQS, 16)
                    g.dma_start(dbg_hact.ap(),
                                hact_d.ap()[0:2, :, :, 0:512]).then_inc(sQS, 16)
                    cnt["QS"] += 112
                    g.wait_ge(sQS, cnt["QS"])

        with nc.Block() as blk:
            @blk.sync
            def _(s):
                s.wait_ge(sQO, cnt["QO"])

        for k, vv in cnt.items():
            assert vv < 60000, (k, vv)
    return nc


def kernel(**inputs):
    x = np.asarray(inputs["L0_attn_x"])
    bf = ml_dtypes.bfloat16

    def f32c(a):
        return np.ascontiguousarray(np.asarray(a), dtype=np.float32)

    wq = f32c(inputs["L0_attn_Wq"]); wk = f32c(inputs["L0_attn_Wk"])
    wv = f32c(inputs["L0_attn_Wv"]); wa = f32c(inputs["L0_attn_Wa"])
    wg = f32c(inputs["L0_attn_Wg"]); wo = f32c(inputs["L0_attn_Wo"])
    w1 = f32c(inputs["L0_attn_W1"]); w2 = f32c(inputs["L0_attn_W2"])
    b1 = f32c(inputs["L0_attn_b1"]); b2 = f32c(inputs["L0_attn_b2"])
    l1s = f32c(inputs["L0_attn_ln1_scale"]); l1b = f32c(inputs["L0_attn_ln1_bias"])
    l2s = f32c(inputs["L0_attn_ln2_scale"]); l2b = f32c(inputs["L0_attn_ln2_bias"])
    gns = f32c(inputs["L0_attn_gn_scale"]); gnb = f32c(inputs["L0_attn_gn_bias"])
    gate_w = f32c(inputs["L0_ffn_gate_w"])
    gu = f32c(inputs["L0_ffn_gate_up_w"])
    dw = f32c(inputs["L0_ffn_down_w"])

    guT = np.ascontiguousarray(gu.transpose(0, 2, 1))
    dwT = np.ascontiguousarray(dw.transpose(0, 2, 1))
    gwT = np.ascontiguousarray(gate_w.T)
    b1c = np.ascontiguousarray(b1.reshape(32, 128).T)
    b2b = np.ascontiguousarray(np.broadcast_to(b2, (128, D)))
    l1sb = np.ascontiguousarray(np.broadcast_to(l1s, (128, D)))
    l1bb = np.ascontiguousarray(np.broadcast_to(l1b, (128, D)))
    l2sb = np.ascontiguousarray(np.broadcast_to(l2s, (128, D)))
    l2bb = np.ascontiguousarray(np.broadcast_to(l2b, (128, D)))
    gnsr = gns.reshape(1, D)
    gnbr = gnb.reshape(1, D)
    ident = np.eye(128, dtype=bf)
    gmap = np.zeros((128, 4), np.float32)
    for p in range(128):
        gmap[p, p // 32] = 1.0

    in_maps = []
    for c in range(8):
        b = c // 4
        t0 = (c % 4) * T
        onehot = np.zeros((128, 8), np.float32)
        onehot[:, c] = 1.0
        bsel = np.zeros((128, 2), np.float32)
        bsel[:, b] = 1.0
        m = {
            "x": np.ascontiguousarray(x[b, t0:t0 + T, :], dtype=np.float32),
            "wq": wq, "wk": wk, "wv": wv, "wa": wa, "wg": wg, "wo": wo,
            "w1": w1, "w2": w2, "b1c": b1c, "b2b": b2b,
            "l1sb": l1sb, "l1bb": l1bb, "l2sb": l2sb, "l2bb": l2bb,
            "gnsr": gnsr, "gnbr": gnbr, "gwT": gwT, "guT": guT, "dwT": dwT,
            "ident": ident, "gmap": gmap, "onehot": onehot, "bsel": bsel,
        }
        in_maps.append(m)

    dbg = bool(inputs.get("_dbg", False))
    nc = build_nc(dbg=dbg)
    res = run_bass_kernel_spmd(nc, in_maps, core_ids=list(range(8)))
    out = np.empty((2, 16384, D), np.float32)
    for c in range(8):
        b = c // 4
        t0 = (c % 4) * T
        out[b, t0:t0 + T, :] = res.results[c]["y"]
    if dbg:
        return out, res
    return out


if __name__ == "__main__":
    print("building...")
    nc = build_nc()
    print("built ok; instructions:", nc.n_instructions())
